# version 06
# baseline (speedup 1.0000x reference)
import numpy as np
import concourse.bass as bass
import concourse.mybir as mybir
from concourse.bass_utils import run_bass_kernel_spmd

F32 = mybir.dt.float32
BF16 = mybir.dt.bfloat16
I32 = mybir.dt.int32
AF = mybir.ActivationFunctionType
OP = mybir.AluOpType
AXX = mybir.AxisListType.X

L, D, E, N, RK, II = 4096, 1024, 2048, 16, 64, 3072
TC = 512
NCH = L // TC
NT = TC // 128
ND = D // 128
NE = E // 128
NI = II // 128
QA = 2.0 ** 15 - 1.0

ENGINES = ("sync", "gpsimd", "tensor", "scalar", "vector")


class K:
    def __init__(self, nc):
        self.nc = nc
        self.streams = {e: [] for e in ENGINES}
        self.sems = {}
        self.cnt = {}

    def sem(self, name):
        self.sems[name] = self.nc.alloc_semaphore(name)
        self.cnt[name] = 0

    def at(self, *names):
        return [(n, self.cnt[n]) for n in names]

    def op(self, eng, fn, waits=None, inc=None, inc_by=1):
        self.streams[eng].append((fn, list(waits or []), inc, inc_by))
        if inc is not None:
            self.cnt[inc] += inc_by

    def emit(self, blk):
        for eng in ENGINES:
            stream = self.streams[eng]
            if not stream:
                continue

            def body(e, stream=stream):
                seen = {}
                for fn, waits, inc, inc_by in stream:
                    for sname, c in waits:
                        if c <= 0 or seen.get(sname, -1) >= c:
                            continue
                        e.wait_ge(self.sems[sname], c)
                        seen[sname] = c
                    inst = fn(e)
                    if inc is not None:
                        inst.then_inc(self.sems[inc], inc_by)
            getattr(blk, eng)(body)


def build_program():
    nc = bass.Bass("TRN2", target_bir_lowering=False, debug=False)

    din = {}
    for name, shape in [
        ("x", [L, D]),
        ("gcol", [128, ND]), ("bcol", [128, ND]),
        ("win", [8, 128, ND * 4 * 128]),
        ("cw", [128, NE * 4]), ("cb", [128, NE]),
        ("wxp", [128, NE * (RK + 2 * N)]),
        ("wdt", [RK, E]),
        ("bdt", [128, NE]), ("alog", [128, NE * N]), ("dpar", [128, NE]),
        ("wout", [128, ND, NE * 128]),
        ("res_scale", [1, 1]),
        ("w_gate", [II, D]), ("w_up", [II, D]), ("w_down", [D, II]),
        ("eye", [128, 128]), ("sel32", [32, 32 * 128]),
    ]:
        din[name] = nc.dram_tensor(name, shape, F32, kind="ExternalInput").ap()
    out_d = nc.dram_tensor("out", [L, D], F32, kind="ExternalOutput").ap()

    x1_sp = nc.dram_tensor("x1_sp", [L, D], F32).ap()
    xn1_sp = nc.dram_tensor("xn1_sp", [L, D], F32).ap()
    sz_sp = nc.dram_tensor("sz_sp", [NCH, NE, 128, TC], BF16).ap()
    up_sp = nc.dram_tensor("up_sp", [NI, 128, L], BF16).ap()
    gate_sp = nc.dram_tensor("gate_sp", [NI, 128, L], BF16).ap()
    g2p_sp = nc.dram_tensor("g2p_sp", [NI, 128, L], BF16).ap()
    o_sp = nc.dram_tensor("o_sp", [NI, 128, L], BF16).ap()
    win_bf = nc.dram_tensor("win_bf", [8, 128, ND * 4 * 128], BF16).ap()
    wout_bf = nc.dram_tensor("wout_bf", [128, ND, NE * 128], BF16).ap()
    wu_sp = nc.dram_tensor("wu_sp", [NI, 128, ND * 128], BF16).ap()
    wg_sp = nc.dram_tensor("wg_sp", [NI, 128, ND * 128], BF16).ap()
    wd_sp = nc.dram_tensor("wd_sp", [ND, 128, NI * 128], BF16).ap()
    cc_in = [nc.dram_tensor(f"cc_in{i}", [1, 2], F32) for i in range(4)]
    cc_out = [nc.dram_tensor(f"cc_out{i}", [1, 2], F32) for i in range(4)]

    sbuf = lambda name, shape, dt=F32: nc.alloc_sbuf_tensor(name, shape, dt)
    eye_f = sbuf("eye_f", [128, 128])
    eye_b = sbuf("eye_b", [128, 128], BF16)
    sel_b = sbuf("sel_b", [32, 32 * 128], BF16)
    ones_f = sbuf("ones_f", [1, 128])
    eps_t = sbuf("eps_t", [128, 1])
    one_t = sbuf("one_t", [128, 1])
    gcol = sbuf("gcolS", [128, ND])
    bcol = sbuf("bcolS", [128, ND])
    A_t = sbuf("A_t", [128, NE * N])
    bdt_t = sbuf("bdt_t", [128, NE])
    Dp_t = sbuf("Dp_t", [128, NE])
    cw_t = sbuf("cw_t", [128, NE * 4])
    cb_t = sbuf("cb_t", [128, NE])
    rs_b = sbuf("rs_b", [128, 1])
    carry = sbuf("carry", [128, NE * N])
    ucar = sbuf("ucar", [128, NE * 3])
    wxz = [sbuf(f"wxz{i}", [128, ND * 4 * 128], BF16) for i in range(2)]
    wxp_t = sbuf("wxp_t", [128, NE * (RK + 2 * N)], BF16)
    wdt_t = sbuf("wdt_t", [64, E], BF16)
    xt = [sbuf(f"xt{i}", [128, D]) for i in range(NT)]
    xn0 = [sbuf(f"xn0{i}", [128, D]) for i in range(NT)]
    xnT = [sbuf(f"xnT{i}", [128, TC], BF16) for i in range(ND)]
    u2 = [sbuf(f"u2_{i}", [128, TC], BF16) for i in range(NE)]
    ustg = [sbuf(f"ustg{i}", [128, TC + 3], BF16) for i in range(2)]
    cacc = sbuf("cacc", [128, TC])
    sq = sbuf("sq", [128, D])
    sp1 = sbuf("sp1", [128, TC])
    st1 = sbuf("st1", [128, 1])
    st2 = sbuf("st2", [128, 1])
    st3 = sbuf("st3", [128, 1])
    dtlo_b = sbuf("dtlo_b", [64, TC], BF16)
    bc_b = sbuf("bc_b", [32, TC], BF16)
    Bb = [sbuf(f"Bb{n}", [128, TC], BF16) for n in range(N)]
    Cb = [sbuf(f"Cb{n}", [128, TC], BF16) for n in range(N)]
    dtt = [sbuf(f"dtt{i}", [128, TC]) for i in range(2)]
    dtu = [sbuf(f"dtu{i}", [128, TC], BF16) for i in range(2)]
    dA_t = [sbuf(f"dA{i}", [128, TC], BF16) for i in range(2)]
    v_t = [sbuf(f"v{i}", [128, TC], BF16) for i in range(2)]
    hCb = sbuf("hCb", [128, N * TC], BF16)
    hC = [hCb[:, n * TC:(n + 1) * TC] for n in range(N)]
    y2t = [sbuf(f"y2t{i}", [128, TC], BF16) for i in range(NE)]
    yacc = sbuf("yacc", [128, TC])
    szst = sbuf("szst", [128, TC], BF16)
    wout_w = [sbuf(f"woutw{i}", [128, NE * 128], BF16) for i in range(2)]
    pev = sbuf("pev", [128, 512])
    mx_x = sbuf("mx_x", [128, 1])
    mxa = sbuf("mxa", [128, 1])
    mxb = sbuf("mxb", [128, 1])
    s11 = sbuf("s11", [1, 24])
    scl = {nm: sbuf("scl_" + nm, [128, 1]) for nm in
           ("rx", "c1", "c2", "c3", "c4", "sg", "so")}
    wub = [sbuf(f"wub{i}", [128, ND * 128], BF16) for i in range(2)]
    wdb = [sbuf(f"wdb{i}", [128, NI * 128], BF16) for i in range(1)]
    fb1 = sbuf("fb1", [128, D], BF16)
    fb2 = sbuf("fb2", [128, TC], BF16)
    fi1 = sbuf("fi1", [128, D], I32)

    pb = [nc.alloc_psum_tensor(f"pb{i}", [128, 512], F32) for i in range(7)]
    pt = nc.alloc_psum_tensor("pt", [128, 1024], BF16)

    k = K(nc)
    sems = ["sd", "sg", "spe", "sa", "sv", "so_", "cc0", "cc1", "cc2", "cc3",
            "bxt0", "bxt1", "bxt2", "bxt3", "bxn0", "bxn1", "bxn2", "bxn3",
            "bfb2", "bszst", "bfb1a", "bfb1b", "bwxz0", "bwxz1",
            "bwout0", "bwout1", "bwub0", "bwub1", "bwdb", "bot", "bcc", "bwc", "sp", "bsp"]
    for s in sems:
        k.sem(s)
    SD, SG, SPE, SA, SV, SO = "sd", "sg", "spe", "sa", "sv", "so_"
    snaps = {}

    def sn(key):
        return snaps.get(key, [])

    def dma(eng, dst, src, waits=None, sem=SD):
        k.op(eng, lambda e, d=dst, s=src: e.dma_start(out=d, in_=s),
             waits=waits, inc=sem, inc_by=16)

    dma("sync", eye_f[:], din["eye"][:, :])
    dma("sync", A_t[:], din["alog"][:, :])
    dma("sync", bdt_t[:], din["bdt"][:, :])
    dma("sync", Dp_t[:], din["dpar"][:, :])
    dma("sync", cw_t[:], din["cw"][:, :])
    dma("sync", cb_t[:], din["cb"][:, :])
    dma("sync", gcol[:], din["gcol"][:, :])
    dma("sync", bcol[:], din["bcol"][:, :])
    dma("sync", s11[0:1, 0:1], din["res_scale"][:, :])
    dma("gpsimd", sel_b[:], din["sel32"][:, :], sem=SG)
    dma("gpsimd", wxp_t[:], din["wxp"][:, :], sem=SG)
    dma("gpsimd", wdt_t[:], din["wdt"][:, :], sem=SG)
    dma("gpsimd", win_bf[:, :, :], din["win"][:, :, :], sem="bwc")
    dma("gpsimd", wout_bf[:, :, :], din["wout"][:, :, :], sem="bwc")

    for t, val in ((eps_t, 1e-5), (one_t, 1.0), (ones_f, 1.0),
                   (carry, 0.0), (ucar, 0.0), (mx_x, 0.0), (mxa, 0.0),
                   (mxb, 0.0), (s11, 0.0)):
        k.op("vector", lambda e, t=t, val=val: e.memset(t[:], val), inc=SV)
    k.op("vector", lambda e: e.tensor_copy(eye_b[:], eye_f[:]), waits=k.at(SD),
         inc=SV)
    k.op("scalar", lambda e: e.activation(A_t[:], A_t[:], AF.Exp), waits=k.at(SD),
         inc=SA)
    k.op("vector", lambda e: e.tensor_scalar(A_t[:], A_t[:], -1.0, None, OP.mult),
         waits=k.at(SA), inc=SV)

    def bcast128(src11, dst):
        k.op("tensor", lambda e, s=src11: e.matmul(pb[6][:, 0:1], ones_f[:, :], s,
                                                   start=True, stop=True),
             waits=k.at(SD, SV, SA) + sn(("bank", 6)), inc=SPE)
        k.op("scalar", lambda e, d=dst: e.copy(d, pb[6][:, 0:1]),
             waits=k.at(SPE), inc=SA)
        snaps[("bank", 6)] = k.at(SA)

    bcast128(s11[0:1, 0:1], rs_b[:])

    def dequant_spill(w_d, O, Dtot, out_sp, idx_order):
        nb = O // 128
        nsec = (Dtot + D - 1) // D
        for b in range(nb):
            for cs in range(nsec):
                dequant_sec(w_d, b, cs, min(D, Dtot - cs * D), out_sp, idx_order)

    def dequant_sec(w_d, b, cs, Din, out_sp, idx_order):
        blocks = Din // 32
        wstage = xt[0]
        if True:
            dma("gpsimd", wstage[:, 0:Din],
                w_d[b * 128:(b + 1) * 128, cs * D:cs * D + Din], sem="bxt0",
                waits=sn(("xt", 0)))
            gw = k.at("bxt0")
            k.op("scalar", lambda e, Din=Din, wstage=wstage: e.activation(
                sq[:, 0:Din], wstage[:, 0:Din], AF.Abs),
                waits=gw + sn("sq"), inc=SA)
            k.op("vector", lambda e, Din=Din, blocks=blocks: e.tensor_reduce(
                pev[:, 0:blocks], sq[:, 0:Din].rearrange("p (c q) -> p c q", q=32),
                AXX, OP.max), waits=k.at(SA) + sn(("pev", 0)), inc=SV)
            k.op("vector", lambda e, blocks=blocks: e.tensor_scalar(
                pev[:, 0:blocks], pev[:, 0:blocks], 1.0 / 7.0, 1e-12,
                OP.mult, OP.max), waits=k.at(SV), inc=SV)
            k.op("vector", lambda e, blocks=blocks: e.reciprocal(
                cacc[:, 0:blocks], pev[:, 0:blocks]), waits=k.at(SV) + sn("cacc"),
                inc=SV)
            rb = cacc[:, 0:blocks].rearrange("p (c o) -> p c o", o=1).broadcast_to(
                [128, blocks, 32])
            sb3 = pev[:, 0:blocks].rearrange("p (c o) -> p c o", o=1).broadcast_to(
                [128, blocks, 32])
            w3 = wstage[:, 0:Din].rearrange("p (c q) -> p c q", q=32)
            i3 = fi1[:, 0:Din].rearrange("p (c q) -> p c q", q=32)
            d3 = sq[:, 0:Din].rearrange("p (c q) -> p c q", q=32)
            k.op("vector", lambda e, w3=w3, rb=rb, i3=i3: e.tensor_tensor(
                i3, w3, rb, OP.mult), waits=k.at(SV) + sn("fi1"), inc=SV)
            snaps[("xt", 0)] = k.at(SV)
            k.op("vector", lambda e, d3=d3, i3=i3, sb3=sb3: e.tensor_tensor(
                d3, i3, sb3, OP.mult), waits=k.at(SV), inc=SV)
            snaps["fi1"] = k.at(SV)
            snaps["cacc"] = k.at(SV)
            snaps[("pev", 0)] = k.at(SV)
            for jb in range(Din // 128):
                k.op("vector", lambda e, jb=jb: e.tensor_copy(
                    fb1[:, 0:128], sq[:, jb * 128:(jb + 1) * 128]),
                    waits=k.at(SV) + sn("fb1"), inc=SV)
                k.op("tensor", lambda e: e.transpose(pt[:, 0:128], fb1[:, 0:128],
                                                     eye_b[:]),
                     waits=k.at(SV) + sn(("bank", "pt", 0)), inc=SPE)
                snaps["fb1"] = k.at(SPE)
                k.op("scalar", lambda e: e.copy(fb2[:, 0:128], pt[:, 0:128]),
                     waits=k.at(SPE) + sn("fb2"), inc=SA)
                snaps[("bank", "pt", 0)] = k.at(SA)
                jbg = cs * (D // 128) + jb
                dst = out_sp[b, :, jbg * 128:(jbg + 1) * 128]
                dma("sync", dst, fb2[:, 0:128], waits=k.at(SA), sem="bfb2")
                snaps["fb2"] = k.at("bfb2")
            snaps["sq"] = k.at(SV)

    dequant_spill(din["w_up"], II, D, wu_sp, 0)
    dequant_spill(din["w_gate"], II, D, wg_sp, 0)
    dequant_spill(din["w_down"], D, II, wd_sp, 1)

    def layernorm_t(src, dst, extra=()):
        w = list(extra)
        k.op("vector", lambda e: e.tensor_reduce(st1[:], src, AXX, OP.add),
             waits=w + k.at(SA), inc=SV)
        k.op("scalar", lambda e: e.activation(sq[:, 0:D], src, AF.Square),
             waits=w + sn("sq") + k.at(SV), inc=SA)
        k.op("vector", lambda e: e.tensor_reduce(st2[:], sq[:, 0:D], AXX, OP.add),
             waits=k.at(SA), inc=SV)
        snaps["sq"] = k.at(SV)
        k.op("vector", lambda e: e.tensor_scalar(st1[:], st1[:], 1.0 / D, None,
                                                 OP.mult), waits=k.at(SV), inc=SV)
        k.op("vector", lambda e: e.tensor_scalar(st2[:], st2[:], 1.0 / D, None,
                                                 OP.mult), waits=k.at(SV), inc=SV)
        k.op("vector", lambda e: e.tensor_mul(st3[:], st1[:], st1[:]),
             waits=k.at(SV), inc=SV)
        k.op("vector", lambda e: e.tensor_sub(st2[:], st2[:], st3[:]),
             waits=k.at(SV), inc=SV)
        k.op("scalar", lambda e: e.activation(st2[:], st2[:], AF.Ln, bias=eps_t[:]),
             waits=k.at(SV), inc=SA)
        k.op("scalar", lambda e: e.activation(st2[:], st2[:], AF.Exp, scale=-0.5),
             waits=k.at(SA), inc=SA)
        k.op("vector", lambda e: e.tensor_scalar(dst, src, st1[:], st2[:],
                                                 OP.subtract, OP.mult),
             waits=k.at(SA), inc=SV)

    def cross_part_max(src_col, col):
        k.op("vector", lambda e: e.memset(sq[:, 0:128], 0.0),
             waits=k.at(SV, SA) + sn("sq"), inc=SV)
        k.op("vector", lambda e: e.tensor_copy(sq[:, 0:1], src_col),
             waits=k.at(SV), inc=SV)
        k.op("tensor", lambda e: e.transpose(pb[6][:, 0:128], sq[:, 0:128],
                                             eye_f[:]),
             waits=k.at(SV) + sn(("bank", 6)), inc=SPE)
        k.op("vector", lambda e, col=col: e.tensor_reduce(
            s11[0:1, col:col + 1], pb[6][0:1, 0:128], AXX, OP.max),
            waits=k.at(SPE), inc=SV)
        snaps[("bank", 6)] = k.at(SV)
        snaps["sq"] = k.at(SPE)

    x_v = din["x"].rearrange("(c t p) d -> c t p d", t=NT, p=128)
    x1v = x1_sp.rearrange("(c t p) d -> c t p d", t=NT, p=128)
    xn1v = xn1_sp.rearrange("(c t p) d -> c t p d", t=NT, p=128)
    outv = out_d.rearrange("(c t p) d -> c t p d", t=NT, p=128)

    for c in range(NCH):
        for it in range(NT):
            dma("sync", xt[it][:], x_v[c, it], waits=sn(("xt", it)),
                sem=f"bxt{it}")
        ldx = k.at("bxt0", "bxt1", "bxt2", "bxt3")
        for it in range(NT):
            layernorm_t(xt[it][:], xn0[it][:],
                        extra=ldx + k.at(SD) + sn(("xn0", it)))
            k.op("vector", lambda e, it=it: e.tensor_add(xt[it][:], xt[it][:],
                                                         xn0[it][:]),
                 waits=k.at(SV), inc=SV)
            layernorm_t(xn0[it][:], xn0[it][:])
        for jd in range(ND):
            for it in range(NT):
                k.op("tensor", lambda e, it=it, jd=jd: e.transpose(
                    pb[jd % 2][:, 0:128], xn0[it][:, jd * 128:(jd + 1) * 128],
                    eye_f[:]),
                    waits=k.at(SV) + sn(("bank", jd % 2)), inc=SPE)
                k.op("vector", lambda e, it=it, jd=jd: e.tensor_scalar(
                    xnT[jd][:, it * 128:(it + 1) * 128], pb[jd % 2][:, 0:128],
                    gcol[:, jd:jd + 1], bcol[:, jd:jd + 1], OP.mult, OP.add),
                    waits=k.at(SPE) + (sn(("xnT", jd)) if it == 0 else []),
                    inc=SV)
                snaps[("bank", jd % 2)] = k.at(SV)
        def xz_epilogue(c, io, bank, bi):
            if io < NE:
                ke = io
                us = ustg[ke % 2]
                k.op("vector", lambda e, ke=ke, us=us: e.tensor_copy(
                    us[:, 0:3], ucar[:, ke * 3:ke * 3 + 3]),
                    waits=sn(("ustg", ke % 2)), inc=SV)
                k.op("scalar", lambda e, us=us, bank=bank: e.copy(
                    us[:, 3:TC + 3], bank[:, 0:TC]), waits=k.at(SPE, SV), inc=SA)
                snaps[("bank", bi)] = k.at(SA)
                k.op("vector", lambda e, ke=ke, us=us: e.tensor_scalar(
                    cacc[:], us[:, 0:TC], cw_t[:, ke * 4:ke * 4 + 1], None,
                    OP.mult), waits=k.at(SA) + sn("cacc"), inc=SV)
                for kk in range(1, 4):
                    k.op("vector", lambda e, ke=ke, kk=kk, us=us:
                         e.scalar_tensor_tensor(
                             cacc[:], us[:, kk:kk + TC],
                             cw_t[:, ke * 4 + kk:ke * 4 + kk + 1], cacc[:],
                             OP.mult, OP.add), waits=k.at(SV), inc=SV)
                k.op("vector", lambda e, ke=ke, us=us: e.tensor_copy(
                    ucar[:, ke * 3:ke * 3 + 3], us[:, TC:TC + 3]),
                    waits=k.at(SV), inc=SV)
                snaps[("ustg", ke % 2)] = k.at(SV)
                k.op("scalar", lambda e, ke=ke: e.activation(
                    u2[ke][:], cacc[:], AF.Silu, bias=cb_t[:, ke:ke + 1]),
                    waits=k.at(SV) + sn(("u2", ke)), inc=SA)
                snaps["cacc"] = k.at(SA)
            else:
                ke = io - NE
                k.op("scalar", lambda e, ke=ke, bank=bank: e.activation(
                    szst[:], bank[:, 0:TC], AF.Silu),
                    waits=k.at(SPE) + sn("szst"), inc=SA)
                snaps[("bank", bi)] = k.at(SA)
                dma("sync", sz_sp[c, ke], szst[:], waits=k.at(SA), sem="bszst")
                snaps["szst"] = k.at("bszst")

        for iog in range(8):
            wpc = wxz[iog % 2]
            dma("sync", wpc[:], win_bf[iog],
                waits=sn(("wxz", iog % 2)) + k.at("bwc"), sem=f"bwxz{iog % 2}")
            gwx = k.at(f"bwxz{iog % 2}")
            for iol in range(4):
                io = iog * 4 + iol
                bi = 2 + io % 4
                bank = pb[bi]
                for kd in range(ND):
                    lhs = wpc[:, (kd * 4 + iol) * 128:(kd * 4 + iol) * 128 + 128]
                    k.op("tensor", lambda e, lhs=lhs, bank=bank, kd=kd: e.matmul(
                        bank[:, 0:TC], lhs, xnT[kd][:, 0:TC],
                        start=(kd == 0), stop=(kd == ND - 1)),
                        waits=(gwx + k.at(SV) + sn(("bank", bi))
                               if kd == 0 else []), inc=SPE)
                xz_epilogue(c, io, bank, bi)
            snaps[("wxz", iog % 2)] = k.at(SPE)
        for ke in range(NE):
            k.op("tensor", lambda e, ke=ke: e.matmul(
                pb[6][0:96, 0:TC], wxp_t[:, ke * 96:(ke + 1) * 96],
                u2[ke][:, 0:TC], start=(ke == 0), stop=(ke == NE - 1)),
                waits=(k.at(SA, SG) + sn(("bank", 6)) if ke == 0 else []),
                inc=SPE)
        k.op("scalar", lambda e: e.copy(dtlo_b[:, :], pb[6][0:64, 0:TC]),
             waits=k.at(SPE) + sn("dtlo"), inc=SA)
        k.op("scalar", lambda e: e.copy(bc_b[:, :], pb[6][64:96, 0:TC]),
             waits=k.at(SPE) + sn("bc_b"), inc=SA)
        snaps[("bank", 6)] = k.at(SA)
        for n in range(2 * N):
            dstt = Bb[n] if n < N else Cb[n - N]
            bkn = 6 if n % 2 == 0 else 1
            k.op("tensor", lambda e, n=n, bkn=bkn: e.matmul(
                pb[bkn][:, 0:TC], sel_b[:, n * 128:(n + 1) * 128], bc_b[:, :],
                start=True, stop=True),
                waits=k.at(SV, SG) + sn(("bank", bkn)), inc=SPE)
            k.op("scalar", lambda e, dstt=dstt, bkn=bkn: e.copy(
                dstt[:], pb[bkn][:, 0:TC]),
                 waits=k.at(SPE) + sn(("BC", n)), inc=SA)
            snaps[("bank", bkn)] = k.at(SA)
        snaps["bc_b"] = k.at(SA)
        for ke in range(NE):
            r = ke % 2
            bi = 2 + r
            k.op("tensor", lambda e, ke=ke, bi=bi: e.matmul(
                pb[bi][:, 0:TC], wdt_t[:, ke * 128:(ke + 1) * 128],
                dtlo_b[:, :], start=True, stop=True),
                waits=k.at(SA, SG) + sn(("bank", bi)), inc=SPE)
            bcv = bdt_t[:, ke:ke + 1]
            k.op("scalar", lambda e, bi=bi, bcv=bcv: e.activation(
                sp1[:], pb[bi][:, 0:TC], AF.Abs, bias=bcv),
                waits=k.at(SPE) + sn("sp1"), inc=SA)
            k.op("scalar", lambda e: e.activation(sp1[:], sp1[:], AF.Exp,
                                                  scale=-1.0),
                 waits=k.at(SA), inc=SA)
            k.op("scalar", lambda e: e.activation(sp1[:], sp1[:], AF.Ln,
                                                  bias=one_t[:]),
                 waits=k.at(SA), inc=SA)
            k.op("scalar", lambda e, r=r, bi=bi, bcv=bcv: e.activation(
                dtt[r][:], pb[bi][:, 0:TC], AF.Relu, bias=bcv),
                waits=k.at(SA) + sn(("dtt", r)), inc=SA)
            snaps[("bank", bi)] = k.at(SA)
            k.op("vector", lambda e, r=r: e.tensor_add(dtt[r][:], dtt[r][:],
                                                       sp1[:]),
                 waits=k.at(SA), inc=SV)
            snaps["sp1"] = k.at(SV)
            k.op("vector", lambda e, ke=ke, r=r: e.tensor_mul(
                dtu[r][:], dtt[r][:], u2[ke][:]),
                waits=k.at(SV) + sn(("dtu", r)), inc=SV)
            snaps[("dttr", r)] = k.at(SV)
            for n in range(N):
                rn = n % 2
                k.op("scalar", lambda e, ke=ke, n=n, r=r, rn=rn: e.activation(
                    dA_t[rn][:], dtt[r][:], AF.Exp,
                    scale=A_t[:, ke * N + n:ke * N + n + 1]),
                    waits=sn(("dttr", r)) + sn(("dA", rn)), inc=SA)
                k.op("vector", lambda e, n=n, r=r, rn=rn: e.tensor_mul(
                    v_t[rn][:], dtu[r][:], Bb[n][:]),
                    waits=k.at(SA) + sn(("v", rn)), inc=SV)
                k.op("vector", lambda e, ke=ke, n=n, rn=rn: e.tensor_tensor_scan(
                    hC[n][:], dA_t[rn][:], v_t[rn][:],
                    carry[:, ke * N + n:ke * N + n + 1], OP.mult, OP.add),
                    waits=k.at(SV, SA) + sn(("hC", n)), inc=SV)
                snaps[("dA", rn)] = k.at(SV)
                snaps[("v", rn)] = k.at(SV)
                k.op("vector", lambda e, ke=ke, n=n: e.tensor_copy(
                    carry[:, ke * N + n:ke * N + n + 1], hC[n][:, TC - 1:TC]),
                    waits=k.at(SV), inc=SV)
                k.op("vector", lambda e, n=n: e.tensor_mul(hC[n][:], hC[n][:],
                                                           Cb[n][:]),
                    waits=k.at(SV), inc=SV)
            snaps[("dtt", r)] = k.at(SV)
            snaps[("dtu", r)] = k.at(SV)
            stride = 1
            while stride < N:
                for n in range(0, N, 2 * stride):
                    k.op("vector", lambda e, n=n, s=stride: e.tensor_add(
                        hC[n][:], hC[n][:], hC[n + s][:]), waits=k.at(SV),
                        inc=SV)
                stride *= 2
            for n in range(N):
                snaps[("hC", n)] = k.at(SV)
            dma("sync", szst[:], sz_sp[c, ke], waits=sn("szst"), sem="bszst")
            k.op("vector", lambda e, ke=ke: e.scalar_tensor_tensor(
                yacc[:], u2[ke][:], Dp_t[:, ke:ke + 1], hC[0][:],
                OP.mult, OP.add), waits=k.at(SV, SA) + sn("yacc"), inc=SV)
            k.op("vector", lambda e, ke=ke: e.tensor_mul(y2t[ke][:], yacc[:],
                                                         szst[:]),
                 waits=k.at(SV, "bszst") + sn(("y2t", ke)), inc=SV)
            snaps["yacc"] = k.at(SV)
            snaps["szst"] = k.at(SV)
        for n in range(2 * N):
            snaps[("BC", n)] = k.at(SV)
        for ke in range(NE):
            snaps[("u2", ke)] = k.at(SV, SPE)
        snaps["dtlo"] = k.at(SPE)
        for jd in range(ND):
            bi = 2 + jd % 4
            bank = pb[bi]
            wob = wout_w[jd % 2]
            dma("sync", wob[:], wout_bf[:, jd, :],
                waits=sn(("wout", jd % 2)) + k.at("bwc"), sem=f"bwout{jd % 2}")
            gw = k.at(f"bwout{jd % 2}")
            for ke in range(NE):
                k.op("tensor", lambda e, ke=ke, bank=bank, wob=wob: e.matmul(
                    bank[:, 0:TC], wob[:, ke * 128:(ke + 1) * 128], y2t[ke][:],
                    start=(ke == 0), stop=(ke == NE - 1)),
                    waits=(gw + k.at(SV) + sn(("bank", bi)) if ke == 0 else []),
                    inc=SPE)
            snaps[("wout", jd % 2)] = k.at(SPE)
            k.op("scalar", lambda e, jd=jd, bank=bank: e.copy(
                pev[:, 0:TC], bank[:, 0:TC]),
                waits=k.at(SPE) + sn(("pev", 0)), inc=SA)
            snaps[("bank", bi)] = k.at(SA)
            for it in range(NT):
                bkt = 6 if it % 2 == 0 else 0
                k.op("tensor", lambda e, jd=jd, it=it, bkt=bkt: e.transpose(
                    pb[bkt][:, 0:128],
                    pev[:, it * 128:it * 128 + 128],
                    eye_f[:]),
                    waits=k.at(SA) + sn(("bank", bkt)), inc=SPE)
                k.op("vector", lambda e, jd=jd, it=it, bkt=bkt:
                     e.scalar_tensor_tensor(
                    xt[it][:, jd * 128:(jd + 1) * 128], pb[bkt][:, 0:128],
                    rs_b[:], xt[it][:, jd * 128:(jd + 1) * 128],
                    OP.mult, OP.add), waits=k.at(SPE), inc=SV)
                snaps[("bank", bkt)] = k.at(SV)
            snaps[("pev", 0)] = k.at(SPE)
        for ke in range(NE):
            snaps[("y2t", ke)] = k.at(SPE)
        for it in range(NT):
            dma("sync", x1v[c, it], xt[it][:], waits=k.at(SV), sem=f"bxt{it}")
            layernorm_t(xt[it][:], xn0[it][:])
            k.op("scalar", lambda e, it=it: e.activation(sq[:, 0:D], xn0[it][:],
                                                         AF.Abs),
                 waits=k.at(SV) + sn("sq"), inc=SA)
            k.op("vector", lambda e: e.tensor_reduce(st1[:], sq[:, 0:D], AXX,
                                                     OP.max),
                 waits=k.at(SA), inc=SV)
            snaps["sq"] = k.at(SV)
            k.op("vector", lambda e: e.tensor_max(mx_x[:], mx_x[:], st1[:]),
                 waits=k.at(SV), inc=SV)
            dma("sync", xn1v[c, it], xn0[it][:], waits=k.at(SV),
                sem=f"bxn{it}")
            snaps[("xt", it)] = k.at(f"bxt{it}")
            snaps[("xn0", it)] = k.at(f"bxn{it}")

    cross_part_max(mx_x[:], 1)
    dma("sync", cc_in[0].ap()[0:1, 0:2], s11[0:1, 1:3], waits=k.at(SV), sem="bcc")
    k.op("gpsimd", lambda e: e.collective_compute(
        "AllReduce", OP.max, replica_groups=[list(range(8))],
        ins=[cc_in[0].ap().opt()], outs=[cc_out[0].ap().opt()]),
        waits=k.at("bcc"), inc="cc0")
    dma("sync", s11[0:1, 2:3], cc_out[0].ap()[0:1, 0:1], waits=k.at("cc0"),
        sem="bcc")
    k.op("vector", lambda e: e.tensor_scalar(
        s11[0:1, 2:3], s11[0:1, 2:3], 1.0 / QA, 1e-12, OP.mult, OP.max),
        waits=k.at("bcc"), inc=SV)
    k.op("vector", lambda e: e.reciprocal(s11[0:1, 3:4], s11[0:1, 2:3]),
         waits=k.at(SV), inc=SV)
    bcast128(s11[0:1, 3:4], scl["rx"][:])

    k.op("sync", lambda e: e.nop(), waits=k.at("bfb2"))
    for c in range(NCH):
        for it in range(NT):
            dma("sync", xt[it][:], xn1v[c, it], waits=sn(("xt", it)),
                sem=f"bxt{it}")
        ldx1 = k.at("bxt0", "bxt1", "bxt2", "bxt3",
                    "bxn0", "bxn1", "bxn2", "bxn3")
        snaps["xiT_ready"] = None
        for it in range(NT):
            k.op("vector", lambda e, it=it: e.tensor_scalar(
                fi1[:, 0:D], xt[it][:], scl["rx"][:], None, OP.mult),
                waits=ldx1 + k.at(SA) + sn("fi1"), inc=SV)
            snaps[("xt", it)] = k.at(SV)
            k.op("vector", lambda e: e.tensor_copy(fb1[:, 0:D], fi1[:, 0:D]),
                 waits=k.at(SV) + sn("fb1"), inc=None)
            k.op("vector", lambda e: e.tensor_copy(st3[:], fb1[:, D - 1:D]),
                 waits=None, inc=SV)
            snaps["fi1"] = k.at(SV)
            for jd in range(ND + 1):
                if jd < ND:
                    k.op("tensor", lambda e, jd=jd: e.transpose(
                        pt[:, (jd % 2) * 128:(jd % 2) * 128 + 128],
                        fb1[:, jd * 128:(jd + 1) * 128], eye_b[:]),
                        waits=k.at(SV) + sn(("bank", "pt", jd % 2)), inc=SPE)
                if jd > 0:
                    pj = jd - 1
                    k.op("scalar", lambda e, pj=pj, it=it: e.copy(
                        xnT[pj][:, it * 128:(it + 1) * 128],
                        pt[:, (pj % 2) * 128:(pj % 2) * 128 + 128]),
                        waits=k.at(SPE) + (sn(("xnT", pj)) if it == 0 else []),
                        inc=SA)
                    snaps[("bank", "pt", pj % 2)] = k.at(SA)
            snaps["fb1"] = k.at(SPE)
            k.op("scalar", lambda e, it=it: e.copy(
                st2[:], xnT[ND - 1][:, it * 128 + 127:it * 128 + 128]),
                waits=None, inc=SA)
        snaps["xiT_ready"] = k.at(SA)
        for im, (wsp, osp, mxt) in enumerate(
                [(wu_sp, up_sp, mxa), (wg_sp, gate_sp, mxb)]):
            for ii in range(NI):
                bi = 2 + ii % 4
                bank = pb[bi]
                wb = wub[ii % 2]
                dma("sync", wb[:], wsp[ii],
                    waits=sn(("wub", ii % 2)), sem=f"bwub{ii % 2}")
                for kd in range(ND):
                    k.op("tensor", lambda e, wb=wb, kd=kd, bank=bank: e.matmul(
                        bank[:, 0:TC], wb[:, kd * 128:(kd + 1) * 128],
                        xnT[kd][:, 0:TC],
                        start=(kd == 0), stop=(kd == ND - 1)),
                        waits=(k.at(f"bwub{ii % 2}") + sn("xiT_ready")
                               + sn(("bank", bi)) if kd == 0 else []),
                        inc=SPE)
                snaps[("wub", ii % 2)] = k.at(SPE)
                k.op("scalar", lambda e, bank=bank: e.copy(fb2[:, 0:TC],
                                                           bank[:, 0:TC]),
                     waits=k.at(SPE) + sn("fb2"), inc=SA)
                snaps[("bank", bi)] = k.at(SA)
                dma("sync", osp[ii, :, c * TC:(c + 1) * TC], fb2[:, 0:TC],
                    waits=k.at(SA), sem="bfb2")
                k.op("scalar", lambda e: e.activation(
                    sp1[:, 0:TC], fb2[:, 0:TC], AF.Abs),
                    waits=k.at(SA, SV), inc=SA)
                snaps["fb2"] = k.at("bfb2", SA)
                k.op("vector", lambda e: e.tensor_reduce(
                    st1[:], sp1[:, 0:TC], AXX, OP.max),
                    waits=k.at(SA), inc=SV)
                k.op("vector", lambda e, mxt=mxt: e.tensor_max(mxt[:], mxt[:],
                                                               st1[:]),
                     waits=k.at(SV), inc=SV)
        for jd in range(ND):
            snaps[("xnT", jd)] = k.at(SPE)

    cross_part_max(mxa[:], 4)
    cross_part_max(mxb[:], 5)
    dma("sync", cc_in[1].ap()[0:1, 0:2], s11[0:1, 4:6], waits=k.at(SV), sem="bcc")
    k.op("gpsimd", lambda e: e.collective_compute(
        "AllReduce", OP.max, replica_groups=[list(range(8))],
        ins=[cc_in[1].ap().opt()], outs=[cc_out[1].ap().opt()]),
        waits=k.at("bcc"), inc="cc1")
    dma("sync", s11[0:1, 4:6], cc_out[1].ap()[0:1, 0:2], waits=k.at("cc1"),
        sem="bcc")
    for src, dst in ((4, 6), (5, 7)):
        k.op("vector", lambda e, src=src, dst=dst: e.tensor_mul(
            s11[0:1, dst:dst + 1], s11[0:1, src:src + 1], s11[0:1, 2:3]),
            waits=k.at("bcc", SV), inc=SV)
        k.op("vector", lambda e, dst=dst: e.tensor_scalar(
            s11[0:1, dst:dst + 1], s11[0:1, dst:dst + 1], 1.0 / QA, 1e-12,
            OP.mult, OP.max), waits=k.at(SV), inc=SV)
    k.op("vector", lambda e: e.reciprocal(s11[0:1, 8:9], s11[0:1, 6:7]),
         waits=k.at(SV), inc=SV)
    k.op("vector", lambda e: e.tensor_mul(s11[0:1, 8:9], s11[0:1, 8:9],
                                          s11[0:1, 2:3]), waits=k.at(SV), inc=SV)
    k.op("vector", lambda e: e.reciprocal(s11[0:1, 9:10], s11[0:1, 7:8]),
         waits=k.at(SV), inc=SV)
    k.op("vector", lambda e: e.tensor_mul(s11[0:1, 9:10], s11[0:1, 9:10],
                                          s11[0:1, 2:3]), waits=k.at(SV), inc=SV)
    bcast128(s11[0:1, 8:9], scl["c2"][:])
    bcast128(s11[0:1, 9:10], scl["c1"][:])
    bcast128(s11[0:1, 7:8], scl["sg"][:])

    k.op("vector", lambda e: e.memset(mxa[:], 0.0), waits=k.at(SV), inc=SV)
    k.op("sync", lambda e: e.nop(), waits=k.at("bfb2"))
    otiles = hC + y2t
    for c in range(NCH):
        for ii in range(NI):
            dma("sync", otiles[ii][:], gate_sp[ii, :, c * TC:(c + 1) * TC],
                waits=sn(("ot", ii)) + k.at("bsp"), sem="bot")
        ldg = k.at("bot")
        for ii in range(NI):
            ot = otiles[ii]
            k.op("vector", lambda e, ot=ot: e.tensor_scalar(
                fi1[:, 0:TC], ot[:], scl["c1"][:], None, OP.mult),
                waits=ldg + k.at(SA) + sn("fi1"), inc=SV)
            k.op("scalar", lambda e: e.activation(sp1[:, 0:TC], fi1[:, 0:TC],
                                                  AF.Sigmoid, scale=scl["sg"][:]),
                 waits=k.at(SV) + sn("ff1"), inc=SA)
            k.op("vector", lambda e: e.tensor_scalar(
                fi1[:, TC:2 * TC], sp1[:, 0:TC], 65536.0, 65535.0,
                OP.mult, OP.min), waits=k.at(SA), inc=SV)
            snaps["ff1"] = k.at(SV)
            k.op("vector", lambda e, ot=ot: e.tensor_tensor(
                ot[:], fi1[:, 0:TC], fi1[:, TC:2 * TC], OP.mult),
                waits=k.at(SV), inc=SV)
            snaps["fi1"] = k.at(SV)
            dma("sync", g2p_sp[ii, :, c * TC:(c + 1) * TC], ot[:],
                waits=k.at(SV), sem="bsp")
            k.op("scalar", lambda e, ot=ot: e.activation(sp1[:, 0:TC],
                                                  ot[:], AF.Abs),
                 waits=k.at(SV, SA), inc=SA)
            snaps[("ot", ii)] = k.at("bot", "bsp", SA)
            k.op("vector", lambda e: e.tensor_reduce(st1[:], sp1[:, 0:TC],
                                                     AXX, OP.max),
                 waits=k.at(SA), inc=SV)
            k.op("vector", lambda e: e.tensor_max(mxa[:], mxa[:], st1[:]),
                 waits=k.at(SV), inc=SV)

    cross_part_max(mxa[:], 10)
    dma("sync", cc_in[2].ap()[0:1, 0:2], s11[0:1, 10:12], waits=k.at(SV), sem="bcc")
    k.op("gpsimd", lambda e: e.collective_compute(
        "AllReduce", OP.max, replica_groups=[list(range(8))],
        ins=[cc_in[2].ap().opt()], outs=[cc_out[2].ap().opt()]),
        waits=k.at("bcc"), inc="cc2")
    dma("sync", s11[0:1, 10:11], cc_out[2].ap()[0:1, 0:1], waits=k.at("cc2"),
        sem="bcc")
    k.op("vector", lambda e: e.tensor_scalar(
        s11[0:1, 11:12], s11[0:1, 7:8], 1.0 / 65536.0, None, OP.mult),
        waits=k.at("bcc", SV), inc=SV)
    k.op("vector", lambda e: e.tensor_mul(s11[0:1, 12:13], s11[0:1, 11:12],
                                          s11[0:1, 10:11]),
         waits=k.at(SV), inc=SV)
    k.op("vector", lambda e: e.tensor_scalar(
        s11[0:1, 12:13], s11[0:1, 12:13], 1.0 / QA, 1e-12, OP.mult, OP.max),
        waits=k.at(SV), inc=SV)
    k.op("vector", lambda e: e.reciprocal(s11[0:1, 13:14], s11[0:1, 12:13]),
         waits=k.at(SV), inc=SV)
    k.op("vector", lambda e: e.tensor_mul(s11[0:1, 13:14], s11[0:1, 13:14],
                                          s11[0:1, 11:12]),
         waits=k.at(SV), inc=SV)
    bcast128(s11[0:1, 13:14], scl["c3"][:])

    k.op("vector", lambda e: e.memset(mxb[:], 0.0), waits=k.at(SV), inc=SV)
    k.op("sync", lambda e: e.nop(), waits=k.at("bfb2", "bot", "bsp"))
    for c in range(NCH):
        for half in range(2):
            i0 = half * 12
            for j in range(12):
                ii = i0 + j
                dma("sync", otiles[j][:], g2p_sp[ii, :, c * TC:(c + 1) * TC],
                    waits=sn(("ot", j)) + k.at("bsp"), sem="bot")
                dma("sync", otiles[12 + j][:],
                    up_sp[ii, :, c * TC:(c + 1) * TC],
                    waits=sn(("ot", 12 + j)), sem="bot")
            ldg = k.at("bot")
            for j in range(12):
                ii = i0 + j
                g2b, upb = otiles[j], otiles[12 + j]
                k.op("vector", lambda e, g2b=g2b: e.tensor_scalar(
                    fi1[:, 0:TC], g2b[:], scl["c3"][:], None, OP.mult),
                    waits=ldg + k.at(SA) + sn("fi1"), inc=SV)
                k.op("vector", lambda e, upb=upb: e.tensor_scalar(
                    fi1[:, TC:2 * TC], upb[:], scl["c2"][:], None,
                    OP.mult), waits=k.at(SV) + sn("fi1b"), inc=SV)
                k.op("vector", lambda e, g2b=g2b: e.tensor_tensor(
                    g2b[:], fi1[:, 0:TC], fi1[:, TC:2 * TC], OP.mult),
                    waits=k.at(SV), inc=SV)
                snaps["fi1"] = k.at(SV)
                snaps["fi1b"] = k.at(SV)
                snaps[("ot", 12 + j)] = k.at(SV)
                dma("sync", o_sp[ii, :, c * TC:(c + 1) * TC], g2b[:],
                    waits=k.at(SV), sem="bsp")
                k.op("scalar", lambda e, g2b=g2b: e.activation(
                    sp1[:, 0:TC], g2b[:], AF.Abs),
                    waits=k.at(SV, SA), inc=SA)
                snaps[("ot", j)] = k.at("bot", "bsp", SA)
                k.op("vector", lambda e: e.tensor_reduce(st1[:], sp1[:, 0:TC],
                                                         AXX, OP.max),
                     waits=k.at(SA), inc=SV)
                k.op("vector", lambda e: e.tensor_max(mxb[:], mxb[:], st1[:]),
                     waits=k.at(SV), inc=SV)

    cross_part_max(mxb[:], 14)
    dma("sync", cc_in[3].ap()[0:1, 0:2], s11[0:1, 14:16], waits=k.at(SV), sem="bcc")
    k.op("gpsimd", lambda e: e.collective_compute(
        "AllReduce", OP.max, replica_groups=[list(range(8))],
        ins=[cc_in[3].ap().opt()], outs=[cc_out[3].ap().opt()]),
        waits=k.at("bcc"), inc="cc3")
    dma("sync", s11[0:1, 14:15], cc_out[3].ap()[0:1, 0:1], waits=k.at("cc3"),
        sem="bcc")
    k.op("vector", lambda e: e.tensor_mul(s11[0:1, 15:16], s11[0:1, 12:13],
                                          s11[0:1, 6:7]),
         waits=k.at("bcc", SV), inc=SV)
    k.op("vector", lambda e: e.tensor_mul(s11[0:1, 16:17], s11[0:1, 15:16],
                                          s11[0:1, 14:15]),
         waits=k.at(SV), inc=SV)
    k.op("vector", lambda e: e.tensor_scalar(
        s11[0:1, 16:17], s11[0:1, 16:17], 1.0 / QA, 1e-12, OP.mult, OP.max),
        waits=k.at(SV), inc=SV)
    k.op("vector", lambda e: e.reciprocal(s11[0:1, 17:18], s11[0:1, 16:17]),
         waits=k.at(SV), inc=SV)
    k.op("vector", lambda e: e.tensor_mul(s11[0:1, 17:18], s11[0:1, 17:18],
                                          s11[0:1, 15:16]),
         waits=k.at(SV), inc=SV)
    bcast128(s11[0:1, 17:18], scl["c4"][:])
    bcast128(s11[0:1, 16:17], scl["so"][:])

    k.op("sync", lambda e: e.nop(), waits=k.at("bfb2", "bot", "bsp"))
    for c in range(NCH):
        for it in range(NT):
            dma("sync", xt[it][:], x1v[c, it], waits=sn(("xt", it)),
                sem=f"bxt{it}")
        for ii in range(NI):
            dma("sync", otiles[ii][:], o_sp[ii, :, c * TC:(c + 1) * TC],
                waits=sn(("ot", ii)) + k.at("bsp"), sem="bot")
        ldot = k.at("bot", SA)
        for ii in range(NI):
            ot = otiles[ii]
            k.op("vector", lambda e, ot=ot: e.tensor_scalar(
                fi1[:, 0:TC], ot[:], scl["c4"][:], None, OP.mult),
                waits=ldot + sn("fi1"), inc=SV)
            k.op("vector", lambda e, ot=ot: e.tensor_copy(ot[:], fi1[:, 0:TC]),
                 waits=k.at(SV), inc=SV)
            snaps["fi1"] = k.at(SV)
        for jd in range(ND):
            bi = 2 + jd % 4
            bank = pb[bi]
            wb = wdb[0]
            dma("sync", wb[:], wd_sp[jd], waits=sn(("wdb", 0)), sem="bwdb")
            for ki in range(NI):
                k.op("tensor", lambda e, wb=wb, ki=ki, bank=bank: e.matmul(
                    bank[:, 0:TC], wb[:, ki * 128:(ki + 1) * 128],
                    otiles[ki][:],
                    start=(ki == 0), stop=(ki == NI - 1)),
                    waits=(k.at("bwdb", SV) + sn(("bank", bi)) if ki == 0
                           else []), inc=SPE)
            snaps[("wdb", 0)] = k.at(SPE)
            k.op("scalar", lambda e, jd=jd, bank=bank: e.copy(
                pev[:, 0:TC], bank[:, 0:TC]),
                waits=k.at(SPE) + sn(("pev", 0)), inc=SA)
            snaps[("bank", bi)] = k.at(SA)
            for it in range(NT):
                bkt = 6 if it % 2 == 0 else 0
                k.op("tensor", lambda e, jd=jd, it=it, bkt=bkt: e.transpose(
                    pb[bkt][:, 0:128],
                    pev[:, it * 128:it * 128 + 128],
                    eye_f[:]),
                    waits=k.at(SA) + sn(("bank", bkt)), inc=SPE)
                k.op("vector", lambda e, jd=jd, it=it, bkt=bkt:
                     e.scalar_tensor_tensor(
                    xt[it][:, jd * 128:(jd + 1) * 128], pb[bkt][:, 0:128],
                    scl["so"][:], xt[it][:, jd * 128:(jd + 1) * 128],
                    OP.mult, OP.add),
                    waits=k.at(SPE, SA, "bxt0", "bxt1", "bxt2", "bxt3"),
                    inc=SV)
                snaps[("bank", bkt)] = k.at(SV)
            snaps[("pev", 0)] = k.at(SPE)
        for ii in range(NI):
            snaps[("ot", ii)] = k.at(SPE)
        for it in range(NT):
            dma("sync", outv[c, it], xt[it][:], waits=k.at(SV), sem=SO)
            snaps[("xt", it)] = k.at(SO)

    k.op("sync", lambda e: e.wait_ge(k.sems[SO], k.cnt[SO]))

    with nc.Block() as blk:
        k.emit(blk)
    return nc


_NC_CACHE = {}


def kernel(L0_attn_x, L0_attn_ln_g, L0_attn_ln_b, L0_attn_W_in, L0_attn_conv_w,
           L0_attn_conv_b, L0_attn_W_xp, L0_attn_W_dt, L0_attn_b_dt,
           L0_attn_A_log, L0_attn_D_param, L0_attn_W_out, L0_attn_res_scale,
           L0_ffn_w_gate, L0_ffn_w_up, L0_ffn_w_down):
    B = L0_attn_x.shape[0]
    f32 = np.float32
    asc = np.ascontiguousarray

    gcol = asc(L0_attn_ln_g.reshape(ND, 128).T.astype(f32))
    bcol = asc(L0_attn_ln_b.reshape(ND, 128).T.astype(f32))
    win = asc(L0_attn_W_in.reshape(ND, 128, 8, 4, 128).transpose(2, 1, 0, 3, 4)
              .reshape(8, 128, ND * 4 * 128).astype(f32))
    cw = asc(L0_attn_conv_w.reshape(NE, 128, 4).transpose(1, 0, 2)
             .reshape(128, NE * 4).astype(f32))
    cb = asc(L0_attn_conv_b.reshape(NE, 128).T.astype(f32))
    wxp = asc(L0_attn_W_xp.reshape(NE, 128, RK + 2 * N).transpose(1, 0, 2)
              .reshape(128, NE * (RK + 2 * N)).astype(f32))
    bdt = asc(L0_attn_b_dt.reshape(NE, 128).T.astype(f32))
    alog = asc(L0_attn_A_log.reshape(NE, 128, N).transpose(1, 0, 2)
               .reshape(128, NE * N).astype(f32))
    dpar = asc(L0_attn_D_param.reshape(NE, 128).T.astype(f32))
    wout = asc(L0_attn_W_out.reshape(NE, 128, ND, 128).transpose(1, 2, 0, 3)
               .reshape(128, ND, NE * 128).astype(f32))
    eye = np.eye(128, dtype=f32)
    sel32 = np.zeros((32, 32 * 128), dtype=f32)
    for n in range(32):
        sel32[n, n * 128:(n + 1) * 128] = 1.0

    if "nc" not in _NC_CACHE:
        _NC_CACHE["nc"] = build_program()
    nc = _NC_CACHE["nc"]

    common = dict(gcol=gcol, bcol=bcol, win=win, cw=cw, cb=cb, wxp=wxp,
                  wdt=asc(L0_attn_W_dt.astype(f32)), bdt=bdt, alog=alog,
                  dpar=dpar, wout=wout,
                  res_scale=asc(L0_attn_res_scale.reshape(1, 1).astype(f32)),
                  w_gate=asc(L0_ffn_w_gate.astype(f32)),
                  w_up=asc(L0_ffn_w_up.astype(f32)),
                  w_down=asc(L0_ffn_w_down.astype(f32)),
                  eye=eye, sel32=sel32)
    in_maps = []
    for i in range(B):
        m = dict(common)
        m["x"] = asc(L0_attn_x[i].astype(f32))
        in_maps.append(m)
    res = run_bass_kernel_spmd(nc, in_maps, core_ids=list(range(B)))
    out = np.stack([np.asarray(res.results[i]["out"], dtype=np.float32)
                    for i in range(B)], axis=0)
    return out


# version 09
# speedup vs baseline: 1.0822x; 1.0822x over previous
import numpy as np
import concourse.bass as bass
import concourse.mybir as mybir
from concourse.bass_utils import run_bass_kernel_spmd

F32 = mybir.dt.float32
BF16 = mybir.dt.bfloat16
I32 = mybir.dt.int32
AF = mybir.ActivationFunctionType
OP = mybir.AluOpType
AXX = mybir.AxisListType.X

L, D, E, N, RK, II = 4096, 1024, 2048, 16, 64, 3072
TC = 512
NCH = L // TC
NT = TC // 128
ND = D // 128
NE = E // 128
NI = II // 128
QA = 2.0 ** 15 - 1.0

ENGINES = ("sync", "gpsimd", "tensor", "scalar", "vector")


class K:
    def __init__(self, nc):
        self.nc = nc
        self.streams = {e: [] for e in ENGINES}
        self.sems = {}
        self.cnt = {}

    def sem(self, name):
        self.sems[name] = self.nc.alloc_semaphore(name)
        self.cnt[name] = 0

    def at(self, *names):
        return [(n, self.cnt[n]) for n in names]

    def op(self, eng, fn, waits=None, inc=None, inc_by=1):
        self.streams[eng].append((fn, list(waits or []), inc, inc_by))
        if inc is not None:
            self.cnt[inc] += inc_by

    def emit(self, blk):
        for eng in ENGINES:
            stream = self.streams[eng]
            if not stream:
                continue

            def body(e, stream=stream):
                seen = {}
                for fn, waits, inc, inc_by in stream:
                    for sname, c in waits:
                        if c <= 0 or seen.get(sname, -1) >= c:
                            continue
                        e.wait_ge(self.sems[sname], c)
                        seen[sname] = c
                    inst = fn(e)
                    if inc is not None:
                        inst.then_inc(self.sems[inc], inc_by)
            getattr(blk, eng)(body)


def build_program():
    nc = bass.Bass("TRN2", target_bir_lowering=False, debug=False)

    din = {}
    for name, shape in [
        ("x", [L, D]),
        ("gcol", [128, ND]), ("bcol", [128, ND]),
        ("win", [8, 128, ND * 4 * 128]),
        ("cw", [128, NE * 4]), ("cb", [128, NE]),
        ("wxp", [128, NE * (RK + 2 * N)]),
        ("wdt", [RK, E]),
        ("bdt", [128, NE]), ("alog", [128, NE * N]), ("dpar", [128, NE]),
        ("wout", [128, ND, NE * 128]),
        ("res_scale", [1, 1]),
        ("w_gate", [II, D]), ("w_up", [II, D]), ("w_down", [D, II]),
        ("eye", [128, 128]), ("sel32", [32, 32 * 128]),
    ]:
        din[name] = nc.dram_tensor(name, shape, F32, kind="ExternalInput").ap()
    out_d = nc.dram_tensor("out", [L, D], F32, kind="ExternalOutput").ap()

    x1_sp = nc.dram_tensor("x1_sp", [L, D], F32).ap()
    xn1_sp = nc.dram_tensor("xn1_sp", [L, D], F32).ap()
    sz_sp = nc.dram_tensor("sz_sp", [NCH, NE, 128, TC], BF16).ap()
    up_sp = nc.dram_tensor("up_sp", [NI, 128, L], BF16).ap()
    gate_sp = nc.dram_tensor("gate_sp", [NI, 128, L], BF16).ap()
    g2p_sp = nc.dram_tensor("g2p_sp", [NI, 128, L], BF16).ap()
    o_sp = nc.dram_tensor("o_sp", [NI, 128, L], BF16).ap()
    win_bf = nc.dram_tensor("win_bf", [8, 128, ND * 4 * 128], BF16).ap()
    wout_bf = nc.dram_tensor("wout_bf", [128, ND, NE * 128], BF16).ap()
    wu_sp = nc.dram_tensor("wu_sp", [NI, 128, ND * 128], BF16).ap()
    wg_sp = nc.dram_tensor("wg_sp", [NI, 128, ND * 128], BF16).ap()
    wd_sp = nc.dram_tensor("wd_sp", [ND, 128, NI * 128], BF16).ap()
    cc_in = [nc.dram_tensor(f"cc_in{i}", [1, 2], F32) for i in range(4)]
    cc_out = [nc.dram_tensor(f"cc_out{i}", [1, 2], F32) for i in range(4)]

    sbuf = lambda name, shape, dt=F32: nc.alloc_sbuf_tensor(name, shape, dt)
    eye_f = sbuf("eye_f", [128, 128])
    eye_b = sbuf("eye_b", [128, 128], BF16)
    sel_b = sbuf("sel_b", [32, 32 * 128], BF16)
    ones_f = sbuf("ones_f", [1, 128])
    eps_t = sbuf("eps_t", [128, 1])
    one_t = sbuf("one_t", [128, 1])
    gcol = sbuf("gcolS", [128, ND])
    bcol = sbuf("bcolS", [128, ND])
    A_t = sbuf("A_t", [128, NE * N])
    bdt_t = sbuf("bdt_t", [128, NE])
    Dp_t = sbuf("Dp_t", [128, NE])
    cw_t = sbuf("cw_t", [128, NE * 4])
    cb_t = sbuf("cb_t", [128, NE])
    rs_b = sbuf("rs_b", [128, 1])
    carry = sbuf("carry", [128, NE * N])
    ucar = sbuf("ucar", [128, NE * 3])
    wxz = [sbuf(f"wxz{i}", [128, ND * 4 * 128], BF16) for i in range(2)]
    wxp_t = sbuf("wxp_t", [128, NE * (RK + 2 * N)], BF16)
    wdt_t = sbuf("wdt_t", [64, E], BF16)
    xt = [sbuf(f"xt{i}", [128, D]) for i in range(NT)]
    xn0 = [sbuf(f"xn0{i}", [128, D]) for i in range(NT)]
    xnT = [sbuf(f"xnT{i}", [128, TC], BF16) for i in range(ND)]
    u2 = [sbuf(f"u2_{i}", [128, TC], BF16) for i in range(NE)]
    ustg = [sbuf(f"ustg{i}", [128, TC + 3], BF16) for i in range(2)]
    cacc = sbuf("cacc", [128, TC])
    sq = sbuf("sq", [128, D])
    sp1 = sbuf("sp1", [128, TC])
    st1 = sbuf("st1", [128, 1])
    st2 = sbuf("st2", [128, 1])
    st3 = sbuf("st3", [128, 1])
    dtlo_b = sbuf("dtlo_b", [64, TC], BF16)
    bc_b = sbuf("bc_b", [32, TC], BF16)
    Bb = [sbuf(f"Bb{n}", [128, TC], BF16) for n in range(N)]
    Cb = [sbuf(f"Cb{n}", [128, TC], BF16) for n in range(N)]
    dtt = [sbuf(f"dtt{i}", [128, TC]) for i in range(2)]
    dtu = [sbuf(f"dtu{i}", [128, TC], BF16) for i in range(2)]
    dA_t = [sbuf(f"dA{i}", [128, TC], BF16) for i in range(2)]
    v_t = [sbuf(f"v{i}", [128, TC], BF16) for i in range(2)]
    hCb = sbuf("hCb", [128, N * TC], BF16)
    hC = [hCb[:, n * TC:(n + 1) * TC] for n in range(N)]
    y2t = [sbuf(f"y2t{i}", [128, TC], BF16) for i in range(NE)]
    yacc = sbuf("yacc", [128, TC])
    szst = sbuf("szst", [128, TC], BF16)
    wout_w = [sbuf(f"woutw{i}", [128, NE * 128], BF16) for i in range(2)]
    pev = sbuf("pev", [128, 512])
    flv = sbuf("flv", [128, 1])
    fla = sbuf("fla", [128, 1])
    mx_x = sbuf("mx_x", [128, 1])
    mxa = sbuf("mxa", [128, 1])
    mxb = sbuf("mxb", [128, 1])
    s11 = sbuf("s11", [1, 24])
    scl = {nm: sbuf("scl_" + nm, [128, 1]) for nm in
           ("rx", "c1", "c2", "c3", "c4", "sg", "so")}
    wub = [sbuf(f"wub{i}", [128, ND * 128], BF16) for i in range(2)]
    wdb = [sbuf(f"wdb{i}", [128, NI * 128], BF16) for i in range(1)]
    fb1 = sbuf("fb1", [128, D], BF16)
    fb2 = sbuf("fb2", [128, TC], BF16)
    fi1 = sbuf("fi1", [128, D], I32)

    pb = [nc.alloc_psum_tensor(f"pb{i}", [128, 512], F32) for i in range(7)]
    pt = nc.alloc_psum_tensor("pt", [128, 1024], BF16)

    k = K(nc)
    sems = ["sd", "sg", "spe", "sa", "sv", "so_", "cc0", "cc1", "cc2", "cc3",
            "bxt0", "bxt1", "bxt2", "bxt3", "bxn0", "bxn1", "bxn2", "bxn3",
            "bfb2", "bszst", "bfb1a", "bfb1b", "bwxz0", "bwxz1",
            "bwout0", "bwout1", "bwub0", "bwub1", "bwdb", "bot", "bcc", "bwc", "sp", "bsp"]
    for s in sems:
        k.sem(s)
    SD, SG, SPE, SA, SV, SO = "sd", "sg", "spe", "sa", "sv", "so_"
    snaps = {}

    def sn(key):
        return snaps.get(key, [])

    def dma(eng, dst, src, waits=None, sem=SD):
        k.op(eng, lambda e, d=dst, s=src: e.dma_start(out=d, in_=s),
             waits=waits, inc=sem, inc_by=16)

    dma("sync", eye_f[:], din["eye"][:, :])
    dma("sync", A_t[:], din["alog"][:, :])
    dma("sync", bdt_t[:], din["bdt"][:, :])
    dma("sync", Dp_t[:], din["dpar"][:, :])
    dma("sync", cw_t[:], din["cw"][:, :])
    dma("sync", cb_t[:], din["cb"][:, :])
    dma("sync", gcol[:], din["gcol"][:, :])
    dma("sync", bcol[:], din["bcol"][:, :])
    dma("sync", s11[0:1, 0:1], din["res_scale"][:, :])
    dma("gpsimd", sel_b[:], din["sel32"][:, :], sem=SG)
    dma("gpsimd", wxp_t[:], din["wxp"][:, :], sem=SG)
    dma("gpsimd", wdt_t[:], din["wdt"][:, :], sem=SG)
    dma("gpsimd", win_bf[:, :, :], din["win"][:, :, :], sem="bwc")
    dma("gpsimd", wout_bf[:, :, :], din["wout"][:, :, :], sem="bwc")

    for t, val in ((eps_t, 1e-5), (one_t, 1.0), (ones_f, 1.0),
                   (carry, 0.0), (ucar, 0.0), (mx_x, 0.0), (mxa, 0.0),
                   (mxb, 0.0), (s11, 0.0)):
        k.op("vector", lambda e, t=t, val=val: e.memset(t[:], val), inc=SV)
    k.op("vector", lambda e: e.tensor_copy(eye_b[:], eye_f[:]), waits=k.at(SD),
         inc=SV)
    k.op("scalar", lambda e: e.activation(A_t[:], A_t[:], AF.Exp), waits=k.at(SD),
         inc=SA)
    k.op("vector", lambda e: e.tensor_scalar(A_t[:], A_t[:], -1.0, None, OP.mult),
         waits=k.at(SA), inc=SV)

    def bcast128(src11, dst):
        k.op("tensor", lambda e, s=src11: e.matmul(pb[6][:, 0:1], ones_f[:, :], s,
                                                   start=True, stop=True),
             waits=k.at(SD, SV, SA) + sn(("bank", 6)), inc=SPE)
        k.op("scalar", lambda e, d=dst: e.copy(d, pb[6][:, 0:1]),
             waits=k.at(SPE), inc=SA)
        snaps[("bank", 6)] = k.at(SA)

    bcast128(s11[0:1, 0:1], rs_b[:])

    def dequant_spill(w_d, O, Dtot, out_sp, idx_order):
        nb = O // 128
        nsec = (Dtot + D - 1) // D
        for b in range(nb):
            for cs in range(nsec):
                dequant_sec(w_d, b, cs, min(D, Dtot - cs * D), out_sp, idx_order)

    def dequant_sec(w_d, b, cs, Din, out_sp, idx_order):
        blocks = Din // 32
        wstage = xt[0]
        if True:
            dma("gpsimd", wstage[:, 0:Din],
                w_d[b * 128:(b + 1) * 128, cs * D:cs * D + Din], sem="bxt0",
                waits=sn(("xt", 0)))
            gw = k.at("bxt0")
            k.op("scalar", lambda e, Din=Din, wstage=wstage: e.activation(
                sq[:, 0:Din], wstage[:, 0:Din], AF.Abs),
                waits=gw + sn("sq"), inc=SA)
            k.op("vector", lambda e, Din=Din, blocks=blocks: e.tensor_reduce(
                pev[:, 0:blocks], sq[:, 0:Din].rearrange("p (c q) -> p c q", q=32),
                AXX, OP.max), waits=k.at(SA) + sn(("pev", 0)), inc=SV)
            k.op("vector", lambda e, blocks=blocks: e.tensor_scalar(
                pev[:, 0:blocks], pev[:, 0:blocks], 1.0 / 7.0, 1e-12,
                OP.mult, OP.max), waits=k.at(SV), inc=SV)
            k.op("vector", lambda e, blocks=blocks: e.reciprocal(
                cacc[:, 0:blocks], pev[:, 0:blocks]), waits=k.at(SV) + sn("cacc"),
                inc=SV)
            rb = cacc[:, 0:blocks].rearrange("p (c o) -> p c o", o=1).broadcast_to(
                [128, blocks, 32])
            sb3 = pev[:, 0:blocks].rearrange("p (c o) -> p c o", o=1).broadcast_to(
                [128, blocks, 32])
            w3 = wstage[:, 0:Din].rearrange("p (c q) -> p c q", q=32)
            i3 = fi1[:, 0:Din].rearrange("p (c q) -> p c q", q=32)
            d3 = sq[:, 0:Din].rearrange("p (c q) -> p c q", q=32)
            k.op("vector", lambda e, w3=w3, rb=rb, i3=i3: e.tensor_tensor(
                i3, w3, rb, OP.mult), waits=k.at(SV) + sn("fi1"), inc=SV)
            snaps[("xt", 0)] = k.at(SV)
            k.op("vector", lambda e, d3=d3, i3=i3, sb3=sb3: e.tensor_tensor(
                d3, i3, sb3, OP.mult), waits=k.at(SV), inc=SV)
            snaps["fi1"] = k.at(SV)
            snaps["cacc"] = k.at(SV)
            snaps[("pev", 0)] = k.at(SV)
            for jb in range(Din // 128):
                k.op("vector", lambda e, jb=jb: e.tensor_copy(
                    fb1[:, 0:128], sq[:, jb * 128:(jb + 1) * 128]),
                    waits=k.at(SV) + sn("fb1"), inc=SV)
                k.op("tensor", lambda e: e.transpose(pt[:, 0:128], fb1[:, 0:128],
                                                     eye_b[:]),
                     waits=k.at(SV) + sn(("bank", "pt", 0)), inc=SPE)
                snaps["fb1"] = k.at(SPE)
                k.op("scalar", lambda e: e.copy(fb2[:, 0:128], pt[:, 0:128]),
                     waits=k.at(SPE) + sn("fb2"), inc=SA)
                snaps[("bank", "pt", 0)] = k.at(SA)
                jbg = cs * (D // 128) + jb
                dst = out_sp[b, :, jbg * 128:(jbg + 1) * 128]
                dma("sync", dst, fb2[:, 0:128], waits=k.at(SA), sem="bfb2")
                snaps["fb2"] = k.at("bfb2")
            snaps["sq"] = k.at(SV)

    dequant_spill(din["w_up"], II, D, wu_sp, 0)
    dequant_spill(din["w_gate"], II, D, wg_sp, 0)
    dequant_spill(din["w_down"], D, II, wd_sp, 1)

    def layernorm_t(src, dst, extra=()):
        w = list(extra)
        k.op("vector", lambda e: e.tensor_reduce(st1[:], src, AXX, OP.add),
             waits=w + k.at(SA), inc=SV)
        k.op("scalar", lambda e: e.activation(sq[:, 0:D], src, AF.Square),
             waits=w + sn("sq") + k.at(SV), inc=SA)
        k.op("vector", lambda e: e.tensor_reduce(st2[:], sq[:, 0:D], AXX, OP.add),
             waits=k.at(SA), inc=SV)
        snaps["sq"] = k.at(SV)
        k.op("vector", lambda e: e.tensor_scalar(st1[:], st1[:], 1.0 / D, None,
                                                 OP.mult), waits=k.at(SV), inc=SV)
        k.op("vector", lambda e: e.tensor_scalar(st2[:], st2[:], 1.0 / D, None,
                                                 OP.mult), waits=k.at(SV), inc=SV)
        k.op("vector", lambda e: e.tensor_mul(st3[:], st1[:], st1[:]),
             waits=k.at(SV), inc=SV)
        k.op("vector", lambda e: e.tensor_sub(st2[:], st2[:], st3[:]),
             waits=k.at(SV), inc=SV)
        k.op("scalar", lambda e: e.activation(st2[:], st2[:], AF.Ln, bias=eps_t[:]),
             waits=k.at(SV), inc=SA)
        k.op("scalar", lambda e: e.activation(st2[:], st2[:], AF.Exp, scale=-0.5),
             waits=k.at(SA), inc=SA)
        k.op("vector", lambda e: e.tensor_scalar(dst, src, st1[:], st2[:],
                                                 OP.subtract, OP.mult),
             waits=k.at(SA), inc=SV)

    def cross_part_max(src_col, col):
        k.op("vector", lambda e: e.memset(sq[:, 0:128], 0.0),
             waits=k.at(SV, SA) + sn("sq"), inc=SV)
        k.op("vector", lambda e: e.tensor_copy(sq[:, 0:1], src_col),
             waits=k.at(SV), inc=SV)
        k.op("tensor", lambda e: e.transpose(pb[6][:, 0:128], sq[:, 0:128],
                                             eye_f[:]),
             waits=k.at(SV) + sn(("bank", 6)), inc=SPE)
        k.op("vector", lambda e, col=col: e.tensor_reduce(
            s11[0:1, col:col + 1], pb[6][0:1, 0:128], AXX, OP.max),
            waits=k.at(SPE), inc=SV)
        snaps[("bank", 6)] = k.at(SV)
        snaps["sq"] = k.at(SPE)

    x_v = din["x"].rearrange("(c t p) d -> c t p d", t=NT, p=128)
    x1v = x1_sp.rearrange("(c t p) d -> c t p d", t=NT, p=128)
    xn1v = xn1_sp.rearrange("(c t p) d -> c t p d", t=NT, p=128)
    outv = out_d.rearrange("(c t p) d -> c t p d", t=NT, p=128)

    for c in range(NCH):
        for it in range(NT):
            dma("sync", xt[it][:], x_v[c, it], waits=sn(("xt", it)),
                sem=f"bxt{it}")
        ldx = k.at("bxt0", "bxt1", "bxt2", "bxt3")
        for it in range(NT):
            layernorm_t(xt[it][:], xn0[it][:],
                        extra=ldx + k.at(SD) + sn(("xn0", it)))
            k.op("vector", lambda e, it=it: e.tensor_add(xt[it][:], xt[it][:],
                                                         xn0[it][:]),
                 waits=k.at(SV), inc=SV)
            layernorm_t(xn0[it][:], xn0[it][:])
        for jd in range(ND):
            for it in range(NT):
                k.op("tensor", lambda e, it=it, jd=jd: e.transpose(
                    pb[jd % 2][:, 0:128], xn0[it][:, jd * 128:(jd + 1) * 128],
                    eye_f[:]),
                    waits=k.at(SV) + sn(("bank", jd % 2)), inc=SPE)
                k.op("vector", lambda e, it=it, jd=jd: e.tensor_scalar(
                    xnT[jd][:, it * 128:(it + 1) * 128], pb[jd % 2][:, 0:128],
                    gcol[:, jd:jd + 1], bcol[:, jd:jd + 1], OP.mult, OP.add),
                    waits=k.at(SPE) + (sn(("xnT", jd)) if it == 0 else []),
                    inc=SV)
                snaps[("bank", jd % 2)] = k.at(SV)
        k.op("vector", lambda e: e.tensor_copy(flv[:],
                                               xnT[ND - 1][:, TC - 1:TC]),
             waits=None, inc=SV)
        def xz_epilogue(c, io, bank, bi):
            if io < NE:
                ke = io
                us = ustg[ke % 2]
                k.op("vector", lambda e, ke=ke, us=us: e.tensor_copy(
                    us[:, 0:3], ucar[:, ke * 3:ke * 3 + 3]),
                    waits=sn(("ustg", ke % 2)), inc=SV)
                k.op("scalar", lambda e, us=us, bank=bank: e.copy(
                    us[:, 3:TC + 3], bank[:, 0:TC]), waits=k.at(SPE, SV), inc=SA)
                snaps[("bank", bi)] = k.at(SA)
                k.op("vector", lambda e, ke=ke, us=us: e.tensor_scalar(
                    cacc[:], us[:, 0:TC], cw_t[:, ke * 4:ke * 4 + 1], None,
                    OP.mult), waits=k.at(SA) + sn("cacc"), inc=SV)
                for kk in range(1, 4):
                    k.op("vector", lambda e, ke=ke, kk=kk, us=us:
                         e.scalar_tensor_tensor(
                             cacc[:], us[:, kk:kk + TC],
                             cw_t[:, ke * 4 + kk:ke * 4 + kk + 1], cacc[:],
                             OP.mult, OP.add), waits=k.at(SV), inc=SV)
                k.op("vector", lambda e, ke=ke, us=us: e.tensor_copy(
                    ucar[:, ke * 3:ke * 3 + 3], us[:, TC:TC + 3]),
                    waits=k.at(SV), inc=SV)
                snaps[("ustg", ke % 2)] = k.at(SV)
                k.op("scalar", lambda e, ke=ke: e.activation(
                    u2[ke][:], cacc[:], AF.Silu, bias=cb_t[:, ke:ke + 1]),
                    waits=k.at(SV) + sn(("u2", ke)), inc=SA)
                snaps["cacc"] = k.at(SA)
            else:
                ke = io - NE
                k.op("scalar", lambda e, ke=ke, bank=bank: e.activation(
                    szst[:], bank[:, 0:TC], AF.Silu),
                    waits=k.at(SPE) + sn("szst"), inc=SA)
                snaps[("bank", bi)] = k.at(SA)
                k.op("scalar", lambda e: e.copy(fla[:], szst[:, TC - 1:TC]),
                     waits=None, inc=SA)
                dma("sync", sz_sp[c, ke], szst[:], waits=k.at(SA), sem="bszst")
                snaps["szst"] = k.at("bszst")

        for iog in range(8):
            wpc = wxz[iog % 2]
            dma("sync", wpc[:], win_bf[iog],
                waits=sn(("wxz", iog % 2)) + k.at("bwc"), sem=f"bwxz{iog % 2}")
            gwx = k.at(f"bwxz{iog % 2}")
            for iol in range(4):
                io = iog * 4 + iol
                bi = 2 + io % 4
                bank = pb[bi]
                for kd in range(ND):
                    lhs = wpc[:, (kd * 4 + iol) * 128:(kd * 4 + iol) * 128 + 128]
                    k.op("tensor", lambda e, lhs=lhs, bank=bank, kd=kd: e.matmul(
                        bank[:, 0:TC], lhs, xnT[kd][:, 0:TC],
                        start=(kd == 0), stop=(kd == ND - 1)),
                        waits=(gwx + k.at(SV) + sn(("bank", bi))
                               if kd == 0 else []), inc=SPE)
                xz_epilogue(c, io, bank, bi)
            snaps[("wxz", iog % 2)] = k.at(SPE)
        for ke in range(NE):
            k.op("tensor", lambda e, ke=ke: e.matmul(
                pb[6][0:96, 0:TC], wxp_t[:, ke * 96:(ke + 1) * 96],
                u2[ke][:, 0:TC], start=(ke == 0), stop=(ke == NE - 1)),
                waits=(k.at(SA, SG) + sn(("bank", 6)) if ke == 0 else []),
                inc=SPE)
        k.op("scalar", lambda e: e.copy(dtlo_b[:, :], pb[6][0:64, 0:TC]),
             waits=k.at(SPE) + sn("dtlo"), inc=SA)
        k.op("scalar", lambda e: e.copy(bc_b[:, :], pb[6][64:96, 0:TC]),
             waits=k.at(SPE) + sn("bc_b"), inc=SA)
        k.op("scalar", lambda e: e.copy(fla[0:1, 0:1],
                                        bc_b.bitcast(F32)[0:1, 0:1]),
             waits=None, inc=SA)
        snaps[("bank", 6)] = k.at(SA)
        for n in range(2 * N):
            dstt = Bb[n] if n < N else Cb[n - N]
            bkn = 6 if n % 2 == 0 else 1
            k.op("tensor", lambda e, n=n, bkn=bkn: e.matmul(
                pb[bkn][:, 0:TC], sel_b[:, n * 128:(n + 1) * 128], bc_b[:, :],
                start=True, stop=True),
                waits=k.at(SV, SG) + sn(("bank", bkn)), inc=SPE)
            k.op("scalar", lambda e, dstt=dstt, bkn=bkn: e.copy(
                dstt[:], pb[bkn][:, 0:TC]),
                 waits=k.at(SPE) + sn(("BC", n)), inc=SA)
            snaps[("bank", bkn)] = k.at(SA)
        snaps["bc_b"] = k.at(SA)
        for ke in range(NE):
            r = ke % 2
            bi = 2 + r
            k.op("tensor", lambda e, ke=ke, bi=bi: e.matmul(
                pb[bi][:, 0:TC], wdt_t[:, ke * 128:(ke + 1) * 128],
                dtlo_b[:, :], start=True, stop=True),
                waits=k.at(SA, SG) + sn(("bank", bi)), inc=SPE)
            bcv = bdt_t[:, ke:ke + 1]
            k.op("scalar", lambda e, bi=bi, bcv=bcv: e.activation(
                sp1[:], pb[bi][:, 0:TC], AF.Abs, bias=bcv),
                waits=k.at(SPE) + sn("sp1"), inc=SA)
            k.op("scalar", lambda e: e.activation(sp1[:], sp1[:], AF.Exp,
                                                  scale=-1.0),
                 waits=k.at(SA), inc=SA)
            k.op("scalar", lambda e: e.activation(sp1[:], sp1[:], AF.Ln,
                                                  bias=one_t[:]),
                 waits=k.at(SA), inc=SA)
            k.op("scalar", lambda e, r=r, bi=bi, bcv=bcv: e.activation(
                dtt[r][:], pb[bi][:, 0:TC], AF.Relu, bias=bcv),
                waits=k.at(SA) + sn(("dtt", r)), inc=SA)
            snaps[("bank", bi)] = k.at(SA)
            k.op("vector", lambda e, r=r: e.tensor_add(dtt[r][:], dtt[r][:],
                                                       sp1[:]),
                 waits=k.at(SA), inc=SV)
            snaps["sp1"] = k.at(SV)
            k.op("vector", lambda e, ke=ke, r=r: e.tensor_mul(
                dtu[r][:], dtt[r][:], u2[ke][:]),
                waits=k.at(SV) + sn(("dtu", r)), inc=SV)
            snaps[("dttr", r)] = k.at(SV)
            for n in range(N):
                rn = n % 2
                k.op("scalar", lambda e, ke=ke, n=n, r=r, rn=rn: e.activation(
                    dA_t[rn][:], dtt[r][:], AF.Exp,
                    scale=A_t[:, ke * N + n:ke * N + n + 1]),
                    waits=sn(("dttr", r)) + sn(("dA", rn)), inc=SA)
                k.op("vector", lambda e, n=n, r=r, rn=rn: e.tensor_mul(
                    v_t[rn][:], dtu[r][:], Bb[n][:]),
                    waits=k.at(SA) + sn(("v", rn)), inc=SV)
                k.op("vector", lambda e, ke=ke, n=n, rn=rn: e.tensor_tensor_scan(
                    hC[n][:], dA_t[rn][:], v_t[rn][:],
                    carry[:, ke * N + n:ke * N + n + 1], OP.mult, OP.add),
                    waits=k.at(SV, SA) + sn(("hC", n)), inc=SV)
                snaps[("dA", rn)] = k.at(SV)
                snaps[("v", rn)] = k.at(SV)
                k.op("vector", lambda e, ke=ke, n=n: e.tensor_copy(
                    carry[:, ke * N + n:ke * N + n + 1], hC[n][:, TC - 1:TC]),
                    waits=k.at(SV), inc=SV)
                k.op("vector", lambda e, n=n: e.tensor_mul(hC[n][:], hC[n][:],
                                                           Cb[n][:]),
                    waits=k.at(SV), inc=SV)
            snaps[("dtt", r)] = k.at(SV)
            snaps[("dtu", r)] = k.at(SV)
            stride = 1
            while stride < N:
                for n in range(0, N, 2 * stride):
                    k.op("vector", lambda e, n=n, s=stride: e.tensor_add(
                        hC[n][:], hC[n][:], hC[n + s][:]), waits=k.at(SV),
                        inc=SV)
                stride *= 2
            for n in range(N):
                snaps[("hC", n)] = k.at(SV)
            dma("sync", szst[:], sz_sp[c, ke], waits=sn("szst"), sem="bszst")
            k.op("vector", lambda e, ke=ke: e.scalar_tensor_tensor(
                yacc[:], u2[ke][:], Dp_t[:, ke:ke + 1], hC[0][:],
                OP.mult, OP.add), waits=k.at(SV, SA) + sn("yacc"), inc=SV)
            k.op("vector", lambda e, ke=ke: e.tensor_mul(y2t[ke][:], yacc[:],
                                                         szst[:]),
                 waits=k.at(SV, "bszst") + sn(("y2t", ke)), inc=SV)
            snaps["yacc"] = k.at(SV)
            snaps["szst"] = k.at(SV)
        k.op("vector", lambda e: e.tensor_copy(flv[:],
                                               y2t[NE - 1][:, TC - 1:TC]),
             waits=None, inc=SV)
        for n in range(2 * N):
            snaps[("BC", n)] = k.at(SV)
        for ke in range(NE):
            snaps[("u2", ke)] = k.at(SV, SPE)
        snaps["dtlo"] = k.at(SPE)
        for jd in range(ND):
            bi = 2 + jd % 4
            bank = pb[bi]
            wob = wout_w[jd % 2]
            dma("sync", wob[:], wout_bf[:, jd, :],
                waits=sn(("wout", jd % 2)) + k.at("bwc"), sem=f"bwout{jd % 2}")
            gw = k.at(f"bwout{jd % 2}")
            for ke in range(NE):
                k.op("tensor", lambda e, ke=ke, bank=bank, wob=wob: e.matmul(
                    bank[:, 0:TC], wob[:, ke * 128:(ke + 1) * 128], y2t[ke][:],
                    start=(ke == 0), stop=(ke == NE - 1)),
                    waits=(gw + k.at(SV) + sn(("bank", bi)) if ke == 0 else []),
                    inc=SPE)
            snaps[("wout", jd % 2)] = k.at(SPE)
            k.op("scalar", lambda e, jd=jd, bank=bank: e.copy(
                pev[:, 0:TC], bank[:, 0:TC]),
                waits=k.at(SPE) + sn(("pev", 0)), inc=SA)
            snaps[("bank", bi)] = k.at(SA)
            for it in range(NT):
                bkt = 6 if it % 2 == 0 else 0
                k.op("tensor", lambda e, jd=jd, it=it, bkt=bkt: e.transpose(
                    pb[bkt][:, 0:128],
                    pev[:, it * 128:it * 128 + 128],
                    eye_f[:]),
                    waits=k.at(SA) + sn(("bank", bkt)), inc=SPE)
                k.op("vector", lambda e, jd=jd, it=it, bkt=bkt:
                     e.scalar_tensor_tensor(
                    xt[it][:, jd * 128:(jd + 1) * 128], pb[bkt][:, 0:128],
                    rs_b[:], xt[it][:, jd * 128:(jd + 1) * 128],
                    OP.mult, OP.add), waits=k.at(SPE), inc=SV)
                snaps[("bank", bkt)] = k.at(SV)
            snaps[("pev", 0)] = k.at(SPE)
        for ke in range(NE):
            snaps[("y2t", ke)] = k.at(SPE)
        for it in range(NT):
            k.op("vector", lambda e, it=it: e.tensor_copy(
                flv[:], xt[it][:, D - 1:D]), waits=None, inc=SV)
            dma("sync", x1v[c, it], xt[it][:], waits=k.at(SV), sem=f"bxt{it}")
            layernorm_t(xt[it][:], xn0[it][:])
            k.op("scalar", lambda e, it=it: e.activation(sq[:, 0:D], xn0[it][:],
                                                         AF.Abs),
                 waits=k.at(SV) + sn("sq"), inc=SA)
            k.op("vector", lambda e: e.tensor_reduce(st1[:], sq[:, 0:D], AXX,
                                                     OP.max),
                 waits=k.at(SA), inc=SV)
            snaps["sq"] = k.at(SV)
            k.op("vector", lambda e: e.tensor_max(mx_x[:], mx_x[:], st1[:]),
                 waits=k.at(SV), inc=SV)
            k.op("vector", lambda e, it=it: e.tensor_copy(
                flv[:], xn0[it][:, D - 1:D]), waits=None, inc=SV)
            dma("sync", xn1v[c, it], xn0[it][:], waits=k.at(SV),
                sem=f"bxn{it}")
            snaps[("xt", it)] = k.at(f"bxt{it}")
            snaps[("xn0", it)] = k.at(f"bxn{it}")

    cross_part_max(mx_x[:], 1)
    dma("sync", cc_in[0].ap()[0:1, 0:2], s11[0:1, 1:3], waits=k.at(SV), sem="bcc")
    k.op("gpsimd", lambda e: e.collective_compute(
        "AllReduce", OP.max, replica_groups=[list(range(8))],
        ins=[cc_in[0].ap().opt()], outs=[cc_out[0].ap().opt()]),
        waits=k.at("bcc"), inc="cc0")
    dma("sync", s11[0:1, 2:3], cc_out[0].ap()[0:1, 0:1], waits=k.at("cc0"),
        sem="bcc")
    k.op("vector", lambda e: e.tensor_scalar(
        s11[0:1, 2:3], s11[0:1, 2:3], 1.0 / QA, 1e-12, OP.mult, OP.max),
        waits=k.at("bcc"), inc=SV)
    k.op("vector", lambda e: e.reciprocal(s11[0:1, 3:4], s11[0:1, 2:3]),
         waits=k.at(SV), inc=SV)
    bcast128(s11[0:1, 3:4], scl["rx"][:])

    k.op("sync", lambda e: e.nop(), waits=k.at("bfb2"))
    for c in range(NCH):
        for it in range(NT):
            dma("sync", xt[it][:], xn1v[c, it], waits=sn(("xt", it)),
                sem=f"bxt{it}")
        ldx1 = k.at("bxt0", "bxt1", "bxt2", "bxt3",
                    "bxn0", "bxn1", "bxn2", "bxn3")
        snaps["xiT_ready"] = None
        for it in range(NT):
            k.op("vector", lambda e, it=it: e.tensor_scalar(
                fi1[:, 0:D], xt[it][:], scl["rx"][:], None, OP.mult),
                waits=ldx1 + k.at(SA) + sn("fi1"), inc=SV)
            snaps[("xt", it)] = k.at(SV)
            k.op("vector", lambda e: e.tensor_copy(fb1[:, 0:D], fi1[:, 0:D]),
                 waits=k.at(SV) + sn("fb1"), inc=None)
            k.op("vector", lambda e: e.tensor_copy(st3[:], fb1[:, D - 1:D]),
                 waits=None, inc=SV)
            snaps["fi1"] = k.at(SV)
            for jd in range(ND + 1):
                if jd < ND:
                    k.op("tensor", lambda e, jd=jd: e.transpose(
                        pt[:, (jd % 2) * 128:(jd % 2) * 128 + 128],
                        fb1[:, jd * 128:(jd + 1) * 128], eye_b[:]),
                        waits=k.at(SV) + sn(("bank", "pt", jd % 2)), inc=SPE)
                if jd > 0:
                    pj = jd - 1
                    k.op("scalar", lambda e, pj=pj, it=it: e.copy(
                        xnT[pj][:, it * 128:(it + 1) * 128],
                        pt[:, (pj % 2) * 128:(pj % 2) * 128 + 128]),
                        waits=k.at(SPE) + (sn(("xnT", pj)) if it == 0 else []),
                        inc=SA)
                    snaps[("bank", "pt", pj % 2)] = k.at(SA)
            snaps["fb1"] = k.at(SPE)
            k.op("scalar", lambda e, it=it: e.copy(
                fla[:], xnT[ND - 1][:, it * 128 + 127:it * 128 + 128]),
                waits=None, inc=SA)
        snaps["xiT_ready"] = k.at(SA)
        for im, (wsp, osp, mxt) in enumerate(
                [(wu_sp, up_sp, mxa), (wg_sp, gate_sp, mxb)]):
            for ii in range(NI):
                bi = 2 + ii % 4
                bank = pb[bi]
                wb = wub[ii % 2]
                dma("sync", wb[:], wsp[ii],
                    waits=sn(("wub", ii % 2)), sem=f"bwub{ii % 2}")
                for kd in range(ND):
                    k.op("tensor", lambda e, wb=wb, kd=kd, bank=bank: e.matmul(
                        bank[:, 0:TC], wb[:, kd * 128:(kd + 1) * 128],
                        xnT[kd][:, 0:TC],
                        start=(kd == 0), stop=(kd == ND - 1)),
                        waits=(k.at(f"bwub{ii % 2}") + sn("xiT_ready")
                               + sn(("bank", bi)) if kd == 0 else []),
                        inc=SPE)
                snaps[("wub", ii % 2)] = k.at(SPE)
                k.op("scalar", lambda e, bank=bank: e.copy(fb2[:, 0:TC],
                                                           bank[:, 0:TC]),
                     waits=k.at(SPE) + sn("fb2"), inc=SA)
                snaps[("bank", bi)] = k.at(SA)
                k.op("scalar", lambda e: e.activation(
                    sp1[:, 0:TC], fb2[:, 0:TC], AF.Abs),
                    waits=k.at(SV), inc=SA)
                dma("gpsimd", osp[ii, :, c * TC:(c + 1) * TC], fb2[:, 0:TC],
                    waits=k.at(SA), sem="bfb2")
                snaps["fb2"] = k.at("bfb2", SA)
                k.op("vector", lambda e: e.tensor_reduce(
                    st1[:], sp1[:, 0:TC], AXX, OP.max),
                    waits=k.at(SA), inc=SV)
                k.op("vector", lambda e, mxt=mxt: e.tensor_max(mxt[:], mxt[:],
                                                               st1[:]),
                     waits=k.at(SV), inc=SV)
        for jd in range(ND):
            snaps[("xnT", jd)] = k.at(SPE)

    cross_part_max(mxa[:], 4)
    cross_part_max(mxb[:], 5)
    dma("sync", cc_in[1].ap()[0:1, 0:2], s11[0:1, 4:6], waits=k.at(SV), sem="bcc")
    k.op("gpsimd", lambda e: e.collective_compute(
        "AllReduce", OP.max, replica_groups=[list(range(8))],
        ins=[cc_in[1].ap().opt()], outs=[cc_out[1].ap().opt()]),
        waits=k.at("bcc"), inc="cc1")
    dma("sync", s11[0:1, 4:6], cc_out[1].ap()[0:1, 0:2], waits=k.at("cc1"),
        sem="bcc")
    for src, dst in ((4, 6), (5, 7)):
        k.op("vector", lambda e, src=src, dst=dst: e.tensor_mul(
            s11[0:1, dst:dst + 1], s11[0:1, src:src + 1], s11[0:1, 2:3]),
            waits=k.at("bcc", SV), inc=SV)
        k.op("vector", lambda e, dst=dst: e.tensor_scalar(
            s11[0:1, dst:dst + 1], s11[0:1, dst:dst + 1], 1.0 / QA, 1e-12,
            OP.mult, OP.max), waits=k.at(SV), inc=SV)
    k.op("vector", lambda e: e.reciprocal(s11[0:1, 8:9], s11[0:1, 6:7]),
         waits=k.at(SV), inc=SV)
    k.op("vector", lambda e: e.tensor_mul(s11[0:1, 8:9], s11[0:1, 8:9],
                                          s11[0:1, 2:3]), waits=k.at(SV), inc=SV)
    k.op("vector", lambda e: e.reciprocal(s11[0:1, 9:10], s11[0:1, 7:8]),
         waits=k.at(SV), inc=SV)
    k.op("vector", lambda e: e.tensor_mul(s11[0:1, 9:10], s11[0:1, 9:10],
                                          s11[0:1, 2:3]), waits=k.at(SV), inc=SV)
    bcast128(s11[0:1, 8:9], scl["c2"][:])
    bcast128(s11[0:1, 9:10], scl["c1"][:])
    bcast128(s11[0:1, 7:8], scl["sg"][:])

    k.op("vector", lambda e: e.memset(mxa[:], 0.0), waits=k.at(SV), inc=SV)
    k.op("sync", lambda e: e.nop(), waits=k.at("bfb2"))
    otiles = hC + y2t
    for c in range(NCH):
        for ii in range(NI):
            dma("sync", otiles[ii][:], gate_sp[ii, :, c * TC:(c + 1) * TC],
                waits=sn(("ot", ii)) + k.at("bsp"), sem="bot")
        ldg = k.at("bot")
        for ii in range(NI):
            ot = otiles[ii]
            k.op("vector", lambda e, ot=ot: e.tensor_scalar(
                fi1[:, 0:TC], ot[:], scl["c1"][:], None, OP.mult),
                waits=ldg + k.at(SA) + sn("fi1"), inc=SV)
            k.op("scalar", lambda e: e.activation(sp1[:, 0:TC], fi1[:, 0:TC],
                                                  AF.Sigmoid, scale=scl["sg"][:]),
                 waits=k.at(SV) + sn("ff1"), inc=SA)
            k.op("vector", lambda e: e.tensor_scalar(
                fi1[:, TC:2 * TC], sp1[:, 0:TC], 65536.0, 65535.0,
                OP.mult, OP.min), waits=k.at(SA), inc=SV)
            snaps["ff1"] = k.at(SV)
            k.op("vector", lambda e, ot=ot: e.tensor_tensor(
                ot[:], fi1[:, 0:TC], fi1[:, TC:2 * TC], OP.mult),
                waits=k.at(SV), inc=SV)
            snaps["fi1"] = k.at(SV)
            k.op("vector", lambda e, ot=ot: e.tensor_copy(
                flv[:], ot[:, TC - 1:TC]), waits=None, inc=SV)
            dma("gpsimd", g2p_sp[ii, :, c * TC:(c + 1) * TC], ot[:],
                waits=k.at(SV), sem="bsp")
            k.op("scalar", lambda e, ot=ot: e.activation(sp1[:, 0:TC],
                                                  ot[:], AF.Abs),
                 waits=k.at(SV, SA), inc=SA)
            snaps[("ot", ii)] = k.at("bot", "bsp", SA)
            k.op("vector", lambda e: e.tensor_reduce(st1[:], sp1[:, 0:TC],
                                                     AXX, OP.max),
                 waits=k.at(SA), inc=SV)
            k.op("vector", lambda e: e.tensor_max(mxa[:], mxa[:], st1[:]),
                 waits=k.at(SV), inc=SV)

    cross_part_max(mxa[:], 10)
    dma("sync", cc_in[2].ap()[0:1, 0:2], s11[0:1, 10:12], waits=k.at(SV), sem="bcc")
    k.op("gpsimd", lambda e: e.collective_compute(
        "AllReduce", OP.max, replica_groups=[list(range(8))],
        ins=[cc_in[2].ap().opt()], outs=[cc_out[2].ap().opt()]),
        waits=k.at("bcc"), inc="cc2")
    dma("sync", s11[0:1, 10:11], cc_out[2].ap()[0:1, 0:1], waits=k.at("cc2"),
        sem="bcc")
    k.op("vector", lambda e: e.tensor_scalar(
        s11[0:1, 11:12], s11[0:1, 7:8], 1.0 / 65536.0, None, OP.mult),
        waits=k.at("bcc", SV), inc=SV)
    k.op("vector", lambda e: e.tensor_mul(s11[0:1, 12:13], s11[0:1, 11:12],
                                          s11[0:1, 10:11]),
         waits=k.at(SV), inc=SV)
    k.op("vector", lambda e: e.tensor_scalar(
        s11[0:1, 12:13], s11[0:1, 12:13], 1.0 / QA, 1e-12, OP.mult, OP.max),
        waits=k.at(SV), inc=SV)
    k.op("vector", lambda e: e.reciprocal(s11[0:1, 13:14], s11[0:1, 12:13]),
         waits=k.at(SV), inc=SV)
    k.op("vector", lambda e: e.tensor_mul(s11[0:1, 13:14], s11[0:1, 13:14],
                                          s11[0:1, 11:12]),
         waits=k.at(SV), inc=SV)
    bcast128(s11[0:1, 13:14], scl["c3"][:])

    k.op("vector", lambda e: e.memset(mxb[:], 0.0), waits=k.at(SV), inc=SV)
    k.op("sync", lambda e: e.nop(), waits=k.at("bfb2", "bot", "bsp"))
    for c in range(NCH):
        for half in range(2):
            i0 = half * 12
            for j in range(12):
                ii = i0 + j
                dma("sync", otiles[j][:], g2p_sp[ii, :, c * TC:(c + 1) * TC],
                    waits=sn(("ot", j)) + k.at("bsp"), sem="bot")
                dma("sync", otiles[12 + j][:],
                    up_sp[ii, :, c * TC:(c + 1) * TC],
                    waits=sn(("ot", 12 + j)), sem="bot")
            ldg = k.at("bot")
            for j in range(12):
                ii = i0 + j
                g2b, upb = otiles[j], otiles[12 + j]
                k.op("vector", lambda e, g2b=g2b: e.tensor_scalar(
                    fi1[:, 0:TC], g2b[:], scl["c3"][:], None, OP.mult),
                    waits=ldg + k.at(SA) + sn("fi1"), inc=SV)
                k.op("vector", lambda e, upb=upb: e.tensor_scalar(
                    fi1[:, TC:2 * TC], upb[:], scl["c2"][:], None,
                    OP.mult), waits=k.at(SV) + sn("fi1b"), inc=SV)
                k.op("vector", lambda e, g2b=g2b: e.tensor_tensor(
                    g2b[:], fi1[:, 0:TC], fi1[:, TC:2 * TC], OP.mult),
                    waits=k.at(SV), inc=SV)
                snaps["fi1"] = k.at(SV)
                snaps["fi1b"] = k.at(SV)
                snaps[("ot", 12 + j)] = k.at(SV)
                k.op("vector", lambda e, g2b=g2b: e.tensor_copy(
                    flv[:], g2b[:, TC - 1:TC]), waits=None, inc=SV)
                dma("gpsimd", o_sp[ii, :, c * TC:(c + 1) * TC], g2b[:],
                    waits=k.at(SV), sem="bsp")
                k.op("scalar", lambda e, g2b=g2b: e.activation(
                    sp1[:, 0:TC], g2b[:], AF.Abs),
                    waits=k.at(SV, SA), inc=SA)
                snaps[("ot", j)] = k.at("bot", "bsp", SA)
                k.op("vector", lambda e: e.tensor_reduce(st1[:], sp1[:, 0:TC],
                                                         AXX, OP.max),
                     waits=k.at(SA), inc=SV)
                k.op("vector", lambda e: e.tensor_max(mxb[:], mxb[:], st1[:]),
                     waits=k.at(SV), inc=SV)

    cross_part_max(mxb[:], 14)
    dma("sync", cc_in[3].ap()[0:1, 0:2], s11[0:1, 14:16], waits=k.at(SV), sem="bcc")
    k.op("gpsimd", lambda e: e.collective_compute(
        "AllReduce", OP.max, replica_groups=[list(range(8))],
        ins=[cc_in[3].ap().opt()], outs=[cc_out[3].ap().opt()]),
        waits=k.at("bcc"), inc="cc3")
    dma("sync", s11[0:1, 14:15], cc_out[3].ap()[0:1, 0:1], waits=k.at("cc3"),
        sem="bcc")
    k.op("vector", lambda e: e.tensor_mul(s11[0:1, 15:16], s11[0:1, 12:13],
                                          s11[0:1, 6:7]),
         waits=k.at("bcc", SV), inc=SV)
    k.op("vector", lambda e: e.tensor_mul(s11[0:1, 16:17], s11[0:1, 15:16],
                                          s11[0:1, 14:15]),
         waits=k.at(SV), inc=SV)
    k.op("vector", lambda e: e.tensor_scalar(
        s11[0:1, 16:17], s11[0:1, 16:17], 1.0 / QA, 1e-12, OP.mult, OP.max),
        waits=k.at(SV), inc=SV)
    k.op("vector", lambda e: e.reciprocal(s11[0:1, 17:18], s11[0:1, 16:17]),
         waits=k.at(SV), inc=SV)
    k.op("vector", lambda e: e.tensor_mul(s11[0:1, 17:18], s11[0:1, 17:18],
                                          s11[0:1, 15:16]),
         waits=k.at(SV), inc=SV)
    bcast128(s11[0:1, 17:18], scl["c4"][:])
    bcast128(s11[0:1, 16:17], scl["so"][:])

    k.op("sync", lambda e: e.nop(), waits=k.at("bfb2", "bot", "bsp"))
    for c in range(NCH):
        for it in range(NT):
            dma("sync", xt[it][:], x1v[c, it], waits=sn(("xt", it)),
                sem=f"bxt{it}")
        for ii in range(NI):
            dma("sync", otiles[ii][:], o_sp[ii, :, c * TC:(c + 1) * TC],
                waits=sn(("ot", ii)) + k.at("bsp"), sem="bot")
        ldot = k.at("bot", SA)
        for ii in range(NI):
            ot = otiles[ii]
            k.op("vector", lambda e, ot=ot: e.tensor_scalar(
                fi1[:, 0:TC], ot[:], scl["c4"][:], None, OP.mult),
                waits=ldot + sn("fi1"), inc=SV)
            k.op("vector", lambda e, ot=ot: e.tensor_copy(ot[:], fi1[:, 0:TC]),
                 waits=k.at(SV), inc=SV)
            snaps["fi1"] = k.at(SV)
        for jd in range(ND):
            bi = 2 + jd % 4
            bank = pb[bi]
            wb = wdb[0]
            dma("sync", wb[:], wd_sp[jd], waits=sn(("wdb", 0)), sem="bwdb")
            for ki in range(NI):
                k.op("tensor", lambda e, wb=wb, ki=ki, bank=bank: e.matmul(
                    bank[:, 0:TC], wb[:, ki * 128:(ki + 1) * 128],
                    otiles[ki][:],
                    start=(ki == 0), stop=(ki == NI - 1)),
                    waits=(k.at("bwdb", SV) + sn(("bank", bi)) if ki == 0
                           else []), inc=SPE)
            snaps[("wdb", 0)] = k.at(SPE)
            k.op("scalar", lambda e, jd=jd, bank=bank: e.copy(
                pev[:, 0:TC], bank[:, 0:TC]),
                waits=k.at(SPE) + sn(("pev", 0)), inc=SA)
            snaps[("bank", bi)] = k.at(SA)
            for it in range(NT):
                bkt = 6 if it % 2 == 0 else 0
                k.op("tensor", lambda e, jd=jd, it=it, bkt=bkt: e.transpose(
                    pb[bkt][:, 0:128],
                    pev[:, it * 128:it * 128 + 128],
                    eye_f[:]),
                    waits=k.at(SA) + sn(("bank", bkt)), inc=SPE)
                k.op("vector", lambda e, jd=jd, it=it, bkt=bkt:
                     e.scalar_tensor_tensor(
                    xt[it][:, jd * 128:(jd + 1) * 128], pb[bkt][:, 0:128],
                    scl["so"][:], xt[it][:, jd * 128:(jd + 1) * 128],
                    OP.mult, OP.add),
                    waits=k.at(SPE, SA, "bxt0", "bxt1", "bxt2", "bxt3"),
                    inc=SV)
                snaps[("bank", bkt)] = k.at(SV)
            snaps[("pev", 0)] = k.at(SPE)
        for ii in range(NI):
            snaps[("ot", ii)] = k.at(SPE)
        for it in range(NT):
            dma("sync", outv[c, it], xt[it][:], waits=k.at(SV), sem=SO)
            snaps[("xt", it)] = k.at(SO)

    k.op("sync", lambda e: e.wait_ge(k.sems[SO], k.cnt[SO]))

    with nc.Block() as blk:
        k.emit(blk)
    return nc


_NC_CACHE = {}


def kernel(L0_attn_x, L0_attn_ln_g, L0_attn_ln_b, L0_attn_W_in, L0_attn_conv_w,
           L0_attn_conv_b, L0_attn_W_xp, L0_attn_W_dt, L0_attn_b_dt,
           L0_attn_A_log, L0_attn_D_param, L0_attn_W_out, L0_attn_res_scale,
           L0_ffn_w_gate, L0_ffn_w_up, L0_ffn_w_down):
    B = L0_attn_x.shape[0]
    f32 = np.float32
    asc = np.ascontiguousarray

    gcol = asc(L0_attn_ln_g.reshape(ND, 128).T.astype(f32))
    bcol = asc(L0_attn_ln_b.reshape(ND, 128).T.astype(f32))
    win = asc(L0_attn_W_in.reshape(ND, 128, 8, 4, 128).transpose(2, 1, 0, 3, 4)
              .reshape(8, 128, ND * 4 * 128).astype(f32))
    cw = asc(L0_attn_conv_w.reshape(NE, 128, 4).transpose(1, 0, 2)
             .reshape(128, NE * 4).astype(f32))
    cb = asc(L0_attn_conv_b.reshape(NE, 128).T.astype(f32))
    wxp = asc(L0_attn_W_xp.reshape(NE, 128, RK + 2 * N).transpose(1, 0, 2)
              .reshape(128, NE * (RK + 2 * N)).astype(f32))
    bdt = asc(L0_attn_b_dt.reshape(NE, 128).T.astype(f32))
    alog = asc(L0_attn_A_log.reshape(NE, 128, N).transpose(1, 0, 2)
               .reshape(128, NE * N).astype(f32))
    dpar = asc(L0_attn_D_param.reshape(NE, 128).T.astype(f32))
    wout = asc(L0_attn_W_out.reshape(NE, 128, ND, 128).transpose(1, 2, 0, 3)
               .reshape(128, ND, NE * 128).astype(f32))
    eye = np.eye(128, dtype=f32)
    sel32 = np.zeros((32, 32 * 128), dtype=f32)
    for n in range(32):
        sel32[n, n * 128:(n + 1) * 128] = 1.0

    if "nc" not in _NC_CACHE:
        _NC_CACHE["nc"] = build_program()
    nc = _NC_CACHE["nc"]

    common = dict(gcol=gcol, bcol=bcol, win=win, cw=cw, cb=cb, wxp=wxp,
                  wdt=asc(L0_attn_W_dt.astype(f32)), bdt=bdt, alog=alog,
                  dpar=dpar, wout=wout,
                  res_scale=asc(L0_attn_res_scale.reshape(1, 1).astype(f32)),
                  w_gate=asc(L0_ffn_w_gate.astype(f32)),
                  w_up=asc(L0_ffn_w_up.astype(f32)),
                  w_down=asc(L0_ffn_w_down.astype(f32)),
                  eye=eye, sel32=sel32)
    in_maps = []
    for i in range(B):
        m = dict(common)
        m["x"] = asc(L0_attn_x[i].astype(f32))
        in_maps.append(m)
    res = run_bass_kernel_spmd(nc, in_maps, core_ids=list(range(B)))
    out = np.stack([np.asarray(res.results[i]["out"], dtype=np.float32)
                    for i in range(B)], axis=0)
    return out


# version 10
# speedup vs baseline: 1.0825x; 1.0003x over previous
import numpy as np
import concourse.bass as bass
import concourse.mybir as mybir
from concourse.bass_utils import run_bass_kernel_spmd

F32 = mybir.dt.float32
BF16 = mybir.dt.bfloat16
I32 = mybir.dt.int32
AF = mybir.ActivationFunctionType
OP = mybir.AluOpType
AXX = mybir.AxisListType.X

L, D, E, N, RK, II = 4096, 1024, 2048, 16, 64, 3072
TC = 512
NCH = L // TC
NT = TC // 128
ND = D // 128
NE = E // 128
NI = II // 128
QA = 2.0 ** 15 - 1.0

ENGINES = ("sync", "gpsimd", "tensor", "scalar", "vector")


class K:
    def __init__(self, nc):
        self.nc = nc
        self.streams = {e: [] for e in ENGINES}
        self.sems = {}
        self.cnt = {}

    def sem(self, name):
        self.sems[name] = self.nc.alloc_semaphore(name)
        self.cnt[name] = 0

    def at(self, *names):
        return [(n, self.cnt[n]) for n in names]

    def op(self, eng, fn, waits=None, inc=None, inc_by=1):
        self.streams[eng].append((fn, list(waits or []), inc, inc_by))
        if inc is not None:
            self.cnt[inc] += inc_by

    def emit(self, blk):
        for eng in ENGINES:
            stream = self.streams[eng]
            if not stream:
                continue

            def body(e, stream=stream):
                seen = {}
                for fn, waits, inc, inc_by in stream:
                    for sname, c in waits:
                        if c <= 0 or seen.get(sname, -1) >= c:
                            continue
                        e.wait_ge(self.sems[sname], c)
                        seen[sname] = c
                    inst = fn(e)
                    if inc is not None:
                        inst.then_inc(self.sems[inc], inc_by)
            getattr(blk, eng)(body)


def build_program():
    nc = bass.Bass("TRN2", target_bir_lowering=False, debug=False)

    din = {}
    for name, shape in [
        ("x", [L, D]),
        ("gcol", [128, ND]), ("bcol", [128, ND]),
        ("win", [8, 128, ND * 4 * 128]),
        ("cw", [128, NE * 4]), ("cb", [128, NE]),
        ("wxp", [128, NE * (RK + 2 * N)]),
        ("wdt", [RK, E]),
        ("bdt", [128, NE]), ("alog", [128, NE * N]), ("dpar", [128, NE]),
        ("wout", [128, ND, NE * 128]),
        ("res_scale", [1, 1]),
        ("w_gate", [II, D]), ("w_up", [II, D]), ("w_down", [D, II]),
        ("eye", [128, 128]), ("sel32", [32, 32 * 128]),
    ]:
        din[name] = nc.dram_tensor(name, shape, F32, kind="ExternalInput").ap()
    out_d = nc.dram_tensor("out", [L, D], F32, kind="ExternalOutput").ap()

    x1_sp = nc.dram_tensor("x1_sp", [L, D], F32).ap()
    xn1_sp = nc.dram_tensor("xn1_sp", [L, D], F32).ap()
    sz_sp = nc.dram_tensor("sz_sp", [NCH, NE, 128, TC], BF16).ap()
    up_sp = nc.dram_tensor("up_sp", [NI, 128, L], BF16).ap()
    gate_sp = nc.dram_tensor("gate_sp", [NI, 128, L], BF16).ap()
    g2p_sp = nc.dram_tensor("g2p_sp", [NI, 128, L], BF16).ap()
    o_sp = nc.dram_tensor("o_sp", [NI, 128, L], BF16).ap()
    win_bf = nc.dram_tensor("win_bf", [8, 128, ND * 4 * 128], BF16).ap()
    wout_bf = nc.dram_tensor("wout_bf", [128, ND, NE * 128], BF16).ap()
    wu_sp = nc.dram_tensor("wu_sp", [NI, 128, ND * 128], BF16).ap()
    wg_sp = nc.dram_tensor("wg_sp", [NI, 128, ND * 128], BF16).ap()
    wd_sp = nc.dram_tensor("wd_sp", [ND, 128, NI * 128], BF16).ap()
    cc_in = [nc.dram_tensor(f"cc_in{i}", [1, 2], F32) for i in range(4)]
    cc_out = [nc.dram_tensor(f"cc_out{i}", [1, 2], F32) for i in range(4)]

    sbuf = lambda name, shape, dt=F32: nc.alloc_sbuf_tensor(name, shape, dt)
    eye_f = sbuf("eye_f", [128, 128])
    eye_b = sbuf("eye_b", [128, 128], BF16)
    sel_b = sbuf("sel_b", [32, 32 * 128], BF16)
    ones_f = sbuf("ones_f", [1, 128])
    eps_t = sbuf("eps_t", [128, 1])
    one_t = sbuf("one_t", [128, 1])
    gcol = sbuf("gcolS", [128, ND])
    bcol = sbuf("bcolS", [128, ND])
    A_t = sbuf("A_t", [128, NE * N])
    bdt_t = sbuf("bdt_t", [128, NE])
    Dp_t = sbuf("Dp_t", [128, NE])
    cw_t = sbuf("cw_t", [128, NE * 4])
    cb_t = sbuf("cb_t", [128, NE])
    rs_b = sbuf("rs_b", [128, 1])
    carry = sbuf("carry", [128, NE * N])
    ucar = sbuf("ucar", [128, NE * 3])
    wxz = [sbuf(f"wxz{i}", [128, ND * 4 * 128], BF16) for i in range(2)]
    wxp_t = sbuf("wxp_t", [128, NE * (RK + 2 * N)], BF16)
    wdt_t = sbuf("wdt_t", [64, E], BF16)
    xt = [sbuf(f"xt{i}", [128, D]) for i in range(NT)]
    xn0 = [sbuf(f"xn0{i}", [128, D]) for i in range(NT)]
    xnT = [sbuf(f"xnT{i}", [128, TC], BF16) for i in range(ND)]
    u2 = [sbuf(f"u2_{i}", [128, TC], BF16) for i in range(NE)]
    ustg = [sbuf(f"ustg{i}", [128, TC + 3], BF16) for i in range(2)]
    cacc = sbuf("cacc", [128, TC])
    sq = sbuf("sq", [128, D])
    sp1 = sbuf("sp1", [128, TC])
    st1 = sbuf("st1", [128, 1])
    st2 = sbuf("st2", [128, 1])
    st3 = sbuf("st3", [128, 1])
    dtlo_b = sbuf("dtlo_b", [64, TC], BF16)
    bc_b = sbuf("bc_b", [32, TC], BF16)
    Bb = [sbuf(f"Bb{n}", [128, TC], BF16) for n in range(N)]
    Cb = [sbuf(f"Cb{n}", [128, TC], BF16) for n in range(N)]
    dtt = [sbuf(f"dtt{i}", [128, TC]) for i in range(2)]
    dtu = [sbuf(f"dtu{i}", [128, TC], BF16) for i in range(2)]
    dA_t = [sbuf(f"dA{i}", [128, TC], BF16) for i in range(2)]
    v_t = [sbuf(f"v{i}", [128, TC], BF16) for i in range(2)]
    hCb = sbuf("hCb", [128, N * TC], BF16)
    hC = [hCb[:, n * TC:(n + 1) * TC] for n in range(N)]
    y2t = [sbuf(f"y2t{i}", [128, TC], BF16) for i in range(NE)]
    yacc = sbuf("yacc", [128, TC])
    szst = sbuf("szst", [128, TC], BF16)
    wout_w = [sbuf(f"woutw{i}", [128, NE * 128], BF16) for i in range(2)]
    pev = sbuf("pev", [128, 2 * 512])
    flv = sbuf("flv", [128, 1])
    fla = sbuf("fla", [128, 1])
    mx_x = sbuf("mx_x", [128, 1])
    mxa = sbuf("mxa", [128, 1])
    mxb = sbuf("mxb", [128, 1])
    s11 = sbuf("s11", [1, 24])
    scl = {nm: sbuf("scl_" + nm, [128, 1]) for nm in
           ("rx", "c1", "c2", "c3", "c4", "sg", "so")}
    wub = [sbuf(f"wub{i}", [128, ND * 128], BF16) for i in range(2)]
    wdb = [None]
    wdb[0] = wxz[0][:, 0:NI * 128]
    fb1 = sbuf("fb1", [128, D], BF16)
    fb2 = sbuf("fb2", [128, TC], BF16)
    fi1 = sbuf("fi1", [128, D], I32)

    pb = [nc.alloc_psum_tensor(f"pb{i}", [128, 512], F32) for i in range(7)]
    pt = nc.alloc_psum_tensor("pt", [128, 1024], BF16)

    k = K(nc)
    sems = ["sd", "sg", "spe", "sa", "sv", "so_", "cc0", "cc1", "cc2", "cc3",
            "bxt0", "bxt1", "bxt2", "bxt3", "bxn0", "bxn1", "bxn2", "bxn3",
            "bfb2", "bszst", "bfb1a", "bfb1b", "bwxz0", "bwxz1",
            "bwout0", "bwout1", "bwub0", "bwub1", "bwdb", "bot", "bcc", "bwc", "sp", "bsp"]
    for s in sems:
        k.sem(s)
    SD, SG, SPE, SA, SV, SO = "sd", "sg", "spe", "sa", "sv", "so_"
    snaps = {}

    def sn(key):
        return snaps.get(key, [])

    def dma(eng, dst, src, waits=None, sem=SD):
        k.op(eng, lambda e, d=dst, s=src: e.dma_start(out=d, in_=s),
             waits=waits, inc=sem, inc_by=16)

    dma("sync", eye_f[:], din["eye"][:, :])
    dma("sync", A_t[:], din["alog"][:, :])
    dma("sync", bdt_t[:], din["bdt"][:, :])
    dma("sync", Dp_t[:], din["dpar"][:, :])
    dma("sync", cw_t[:], din["cw"][:, :])
    dma("sync", cb_t[:], din["cb"][:, :])
    dma("sync", gcol[:], din["gcol"][:, :])
    dma("sync", bcol[:], din["bcol"][:, :])
    dma("sync", s11[0:1, 0:1], din["res_scale"][:, :])
    dma("gpsimd", sel_b[:], din["sel32"][:, :], sem=SG)
    dma("gpsimd", wxp_t[:], din["wxp"][:, :], sem=SG)
    dma("gpsimd", wdt_t[:], din["wdt"][:, :], sem=SG)
    dma("gpsimd", win_bf[:, :, :], din["win"][:, :, :], sem="bwc")
    dma("gpsimd", wout_bf[:, :, :], din["wout"][:, :, :], sem="bwc")

    for t, val in ((eps_t, 1e-5), (one_t, 1.0), (ones_f, 1.0),
                   (carry, 0.0), (ucar, 0.0), (mx_x, 0.0), (mxa, 0.0),
                   (mxb, 0.0), (s11, 0.0)):
        k.op("vector", lambda e, t=t, val=val: e.memset(t[:], val), inc=SV)
    k.op("vector", lambda e: e.tensor_copy(eye_b[:], eye_f[:]), waits=k.at(SD),
         inc=SV)
    k.op("scalar", lambda e: e.activation(A_t[:], A_t[:], AF.Exp), waits=k.at(SD),
         inc=SA)
    k.op("vector", lambda e: e.tensor_scalar(A_t[:], A_t[:], -1.0, None, OP.mult),
         waits=k.at(SA), inc=SV)

    def bcast128(src11, dst):
        k.op("tensor", lambda e, s=src11: e.matmul(pb[6][:, 0:1], ones_f[:, :], s,
                                                   start=True, stop=True),
             waits=k.at(SD, SV, SA) + sn(("bank", 6)), inc=SPE)
        k.op("scalar", lambda e, d=dst: e.copy(d, pb[6][:, 0:1]),
             waits=k.at(SPE), inc=SA)
        snaps[("bank", 6)] = k.at(SA)

    bcast128(s11[0:1, 0:1], rs_b[:])

    def dequant_spill(w_d, O, Dtot, out_sp, idx_order):
        nb = O // 128
        nsec = (Dtot + D - 1) // D
        for b in range(nb):
            for cs in range(nsec):
                dequant_sec(w_d, b, cs, min(D, Dtot - cs * D), out_sp, idx_order)

    def dequant_sec(w_d, b, cs, Din, out_sp, idx_order):
        blocks = Din // 32
        wstage = xt[0]
        if True:
            dma("gpsimd", wstage[:, 0:Din],
                w_d[b * 128:(b + 1) * 128, cs * D:cs * D + Din], sem="bxt0",
                waits=sn(("xt", 0)))
            gw = k.at("bxt0")
            k.op("scalar", lambda e, Din=Din, wstage=wstage: e.activation(
                sq[:, 0:Din], wstage[:, 0:Din], AF.Abs),
                waits=gw + sn("sq"), inc=SA)
            k.op("vector", lambda e, Din=Din, blocks=blocks: e.tensor_reduce(
                pev[:, 0:blocks], sq[:, 0:Din].rearrange("p (c q) -> p c q", q=32),
                AXX, OP.max), waits=k.at(SA) + sn(("pev", 0)), inc=SV)
            k.op("vector", lambda e, blocks=blocks: e.tensor_scalar(
                pev[:, 0:blocks], pev[:, 0:blocks], 1.0 / 7.0, 1e-12,
                OP.mult, OP.max), waits=k.at(SV), inc=SV)
            k.op("vector", lambda e, blocks=blocks: e.reciprocal(
                cacc[:, 0:blocks], pev[:, 0:blocks]), waits=k.at(SV) + sn("cacc"),
                inc=SV)
            rb = cacc[:, 0:blocks].rearrange("p (c o) -> p c o", o=1).broadcast_to(
                [128, blocks, 32])
            sb3 = pev[:, 0:blocks].rearrange("p (c o) -> p c o", o=1).broadcast_to(
                [128, blocks, 32])
            w3 = wstage[:, 0:Din].rearrange("p (c q) -> p c q", q=32)
            i3 = fi1[:, 0:Din].rearrange("p (c q) -> p c q", q=32)
            d3 = sq[:, 0:Din].rearrange("p (c q) -> p c q", q=32)
            k.op("vector", lambda e, w3=w3, rb=rb, i3=i3: e.tensor_tensor(
                i3, w3, rb, OP.mult), waits=k.at(SV) + sn("fi1"), inc=SV)
            snaps[("xt", 0)] = k.at(SV)
            k.op("vector", lambda e, d3=d3, i3=i3, sb3=sb3: e.tensor_tensor(
                d3, i3, sb3, OP.mult), waits=k.at(SV), inc=SV)
            snaps["fi1"] = k.at(SV)
            snaps["cacc"] = k.at(SV)
            snaps[("pev", 0)] = k.at(SV)
            for jb in range(Din // 128):
                k.op("vector", lambda e, jb=jb: e.tensor_copy(
                    fb1[:, 0:128], sq[:, jb * 128:(jb + 1) * 128]),
                    waits=k.at(SV) + sn("fb1"), inc=SV)
                k.op("tensor", lambda e: e.transpose(pt[:, 0:128], fb1[:, 0:128],
                                                     eye_b[:]),
                     waits=k.at(SV) + sn(("bank", "pt", 0)), inc=SPE)
                snaps["fb1"] = k.at(SPE)
                k.op("scalar", lambda e: e.copy(fb2[:, 0:128], pt[:, 0:128]),
                     waits=k.at(SPE) + sn("fb2"), inc=SA)
                snaps[("bank", "pt", 0)] = k.at(SA)
                jbg = cs * (D // 128) + jb
                dst = out_sp[b, :, jbg * 128:(jbg + 1) * 128]
                dma("sync", dst, fb2[:, 0:128], waits=k.at(SA), sem="bfb2")
                snaps["fb2"] = k.at("bfb2")
            snaps["sq"] = k.at(SV)

    dequant_spill(din["w_up"], II, D, wu_sp, 0)
    dequant_spill(din["w_gate"], II, D, wg_sp, 0)
    dequant_spill(din["w_down"], D, II, wd_sp, 1)

    def layernorm_t(src, dst, extra=()):
        w = list(extra)
        k.op("vector", lambda e: e.tensor_reduce(st1[:], src, AXX, OP.add),
             waits=w + k.at(SA), inc=SV)
        k.op("scalar", lambda e: e.activation(sq[:, 0:D], src, AF.Square),
             waits=w + sn("sq") + k.at(SV), inc=SA)
        k.op("vector", lambda e: e.tensor_reduce(st2[:], sq[:, 0:D], AXX, OP.add),
             waits=k.at(SA), inc=SV)
        snaps["sq"] = k.at(SV)
        k.op("vector", lambda e: e.tensor_scalar(st1[:], st1[:], 1.0 / D, None,
                                                 OP.mult), waits=k.at(SV), inc=SV)
        k.op("vector", lambda e: e.tensor_scalar(st2[:], st2[:], 1.0 / D, None,
                                                 OP.mult), waits=k.at(SV), inc=SV)
        k.op("vector", lambda e: e.tensor_mul(st3[:], st1[:], st1[:]),
             waits=k.at(SV), inc=SV)
        k.op("vector", lambda e: e.tensor_sub(st2[:], st2[:], st3[:]),
             waits=k.at(SV), inc=SV)
        k.op("scalar", lambda e: e.activation(st2[:], st2[:], AF.Ln, bias=eps_t[:]),
             waits=k.at(SV), inc=SA)
        k.op("scalar", lambda e: e.activation(st2[:], st2[:], AF.Exp, scale=-0.5),
             waits=k.at(SA), inc=SA)
        k.op("vector", lambda e: e.tensor_scalar(dst, src, st1[:], st2[:],
                                                 OP.subtract, OP.mult),
             waits=k.at(SA), inc=SV)

    def cross_part_max(src_col, col):
        k.op("vector", lambda e: e.memset(sq[:, 0:128], 0.0),
             waits=k.at(SV, SA) + sn("sq"), inc=SV)
        k.op("vector", lambda e: e.tensor_copy(sq[:, 0:1], src_col),
             waits=k.at(SV), inc=SV)
        k.op("tensor", lambda e: e.transpose(pb[6][:, 0:128], sq[:, 0:128],
                                             eye_f[:]),
             waits=k.at(SV) + sn(("bank", 6)), inc=SPE)
        k.op("vector", lambda e, col=col: e.tensor_reduce(
            s11[0:1, col:col + 1], pb[6][0:1, 0:128], AXX, OP.max),
            waits=k.at(SPE), inc=SV)
        snaps[("bank", 6)] = k.at(SV)
        snaps["sq"] = k.at(SPE)

    x_v = din["x"].rearrange("(c t p) d -> c t p d", t=NT, p=128)
    x1v = x1_sp.rearrange("(c t p) d -> c t p d", t=NT, p=128)
    xn1v = xn1_sp.rearrange("(c t p) d -> c t p d", t=NT, p=128)
    outv = out_d.rearrange("(c t p) d -> c t p d", t=NT, p=128)

    for c in range(NCH):
        for it in range(NT):
            dma("sync", xt[it][:], x_v[c, it], waits=sn(("xt", it)),
                sem=f"bxt{it}")
        ldx = k.at("bxt0", "bxt1", "bxt2", "bxt3")
        for it in range(NT):
            layernorm_t(xt[it][:], xn0[it][:],
                        extra=ldx + k.at(SD) + sn(("xn0", it)))
            k.op("vector", lambda e, it=it: e.tensor_add(xt[it][:], xt[it][:],
                                                         xn0[it][:]),
                 waits=k.at(SV), inc=SV)
            layernorm_t(xn0[it][:], xn0[it][:])
        for jd in range(ND):
            for it in range(NT):
                k.op("tensor", lambda e, it=it, jd=jd: e.transpose(
                    pb[jd % 2][:, 0:128], xn0[it][:, jd * 128:(jd + 1) * 128],
                    eye_f[:]),
                    waits=k.at(SV) + sn(("bank", jd % 2)), inc=SPE)
                k.op("vector", lambda e, it=it, jd=jd: e.tensor_scalar(
                    xnT[jd][:, it * 128:(it + 1) * 128], pb[jd % 2][:, 0:128],
                    gcol[:, jd:jd + 1], bcol[:, jd:jd + 1], OP.mult, OP.add),
                    waits=k.at(SPE) + (sn(("xnT", jd)) if it == 0 else []),
                    inc=SV)
                snaps[("bank", jd % 2)] = k.at(SV)
        k.op("vector", lambda e: e.tensor_copy(flv[:],
                                               xnT[ND - 1][:, TC - 1:TC]),
             waits=None, inc=SV)
        def xz_epilogue(c, io, bank, bi):
            if io < NE:
                ke = io
                us = ustg[ke % 2]
                k.op("vector", lambda e, ke=ke, us=us: e.tensor_copy(
                    us[:, 0:3], ucar[:, ke * 3:ke * 3 + 3]),
                    waits=sn(("ustg", ke % 2)), inc=SV)
                k.op("scalar", lambda e, us=us, bank=bank: e.copy(
                    us[:, 3:TC + 3], bank[:, 0:TC]), waits=k.at(SPE, SV), inc=SA)
                snaps[("bank", bi)] = k.at(SA)
                k.op("vector", lambda e, ke=ke, us=us: e.tensor_scalar(
                    cacc[:], us[:, 0:TC], cw_t[:, ke * 4:ke * 4 + 1], None,
                    OP.mult), waits=k.at(SA) + sn("cacc"), inc=SV)
                for kk in range(1, 4):
                    k.op("vector", lambda e, ke=ke, kk=kk, us=us:
                         e.scalar_tensor_tensor(
                             cacc[:], us[:, kk:kk + TC],
                             cw_t[:, ke * 4 + kk:ke * 4 + kk + 1], cacc[:],
                             OP.mult, OP.add), waits=k.at(SV), inc=SV)
                k.op("vector", lambda e, ke=ke, us=us: e.tensor_copy(
                    ucar[:, ke * 3:ke * 3 + 3], us[:, TC:TC + 3]),
                    waits=k.at(SV), inc=SV)
                snaps[("ustg", ke % 2)] = k.at(SV)
                k.op("scalar", lambda e, ke=ke: e.activation(
                    u2[ke][:], cacc[:], AF.Silu, bias=cb_t[:, ke:ke + 1]),
                    waits=k.at(SV) + sn(("u2", ke)), inc=SA)
                snaps["cacc"] = k.at(SA)
            else:
                ke = io - NE
                k.op("scalar", lambda e, ke=ke, bank=bank: e.activation(
                    szst[:], bank[:, 0:TC], AF.Silu),
                    waits=k.at(SPE) + sn("szst"), inc=SA)
                snaps[("bank", bi)] = k.at(SA)
                k.op("scalar", lambda e: e.copy(fla[:], szst[:, TC - 1:TC]),
                     waits=None, inc=SA)
                dma("sync", sz_sp[c, ke], szst[:], waits=k.at(SA), sem="bszst")
                snaps["szst"] = k.at("bszst")

        for iog in range(8):
            wpc = wxz[iog % 2]
            dma("sync", wpc[:], win_bf[iog],
                waits=sn(("wxz", iog % 2)) + k.at("bwc"), sem=f"bwxz{iog % 2}")
            gwx = k.at(f"bwxz{iog % 2}")
            for iol in range(4):
                io = iog * 4 + iol
                bi = 2 + io % 4
                bank = pb[bi]
                for kd in range(ND):
                    lhs = wpc[:, (kd * 4 + iol) * 128:(kd * 4 + iol) * 128 + 128]
                    k.op("tensor", lambda e, lhs=lhs, bank=bank, kd=kd: e.matmul(
                        bank[:, 0:TC], lhs, xnT[kd][:, 0:TC],
                        start=(kd == 0), stop=(kd == ND - 1)),
                        waits=(gwx + k.at(SV) + sn(("bank", bi))
                               if kd == 0 else []), inc=SPE)
                xz_epilogue(c, io, bank, bi)
            snaps[("wxz", iog % 2)] = k.at(SPE)
        for ke in range(NE):
            k.op("tensor", lambda e, ke=ke: e.matmul(
                pb[6][0:96, 0:TC], wxp_t[:, ke * 96:(ke + 1) * 96],
                u2[ke][:, 0:TC], start=(ke == 0), stop=(ke == NE - 1)),
                waits=(k.at(SA, SG) + sn(("bank", 6)) if ke == 0 else []),
                inc=SPE)
        k.op("scalar", lambda e: e.copy(dtlo_b[:, :], pb[6][0:64, 0:TC]),
             waits=k.at(SPE) + sn("dtlo"), inc=SA)
        k.op("scalar", lambda e: e.copy(bc_b[:, :], pb[6][64:96, 0:TC]),
             waits=k.at(SPE) + sn("bc_b"), inc=SA)
        k.op("scalar", lambda e: e.copy(fla[0:1, 0:1],
                                        bc_b.bitcast(F32)[0:1, 0:1]),
             waits=None, inc=SA)
        snaps[("bank", 6)] = k.at(SA)
        for n in range(2 * N):
            dstt = Bb[n] if n < N else Cb[n - N]
            bkn = 6 if n % 2 == 0 else 1
            k.op("tensor", lambda e, n=n, bkn=bkn: e.matmul(
                pb[bkn][:, 0:TC], sel_b[:, n * 128:(n + 1) * 128], bc_b[:, :],
                start=True, stop=True),
                waits=k.at(SV, SG) + sn(("bank", bkn)), inc=SPE)
            k.op("scalar", lambda e, dstt=dstt, bkn=bkn: e.copy(
                dstt[:], pb[bkn][:, 0:TC]),
                 waits=k.at(SPE) + sn(("BC", n)), inc=SA)
            snaps[("bank", bkn)] = k.at(SA)
        snaps["bc_b"] = k.at(SA)
        szt = [szst, ustg[0]]
        for ke in range(NE):
            r = ke % 2
            bi = 2 + r
            dma("sync", szt[r][:, 0:TC], sz_sp[c, ke],
                waits=sn(("szr", r)), sem="bszst")
            k.op("tensor", lambda e, ke=ke, bi=bi: e.matmul(
                pb[bi][:, 0:TC], wdt_t[:, ke * 128:(ke + 1) * 128],
                dtlo_b[:, :], start=True, stop=True),
                waits=k.at(SA, SG) + sn(("bank", bi)), inc=SPE)
            bcv = bdt_t[:, ke:ke + 1]
            k.op("scalar", lambda e, bi=bi, bcv=bcv: e.activation(
                sp1[:], pb[bi][:, 0:TC], AF.Abs, bias=bcv),
                waits=k.at(SPE) + sn("sp1"), inc=SA)
            k.op("scalar", lambda e: e.activation(sp1[:], sp1[:], AF.Exp,
                                                  scale=-1.0),
                 waits=k.at(SA), inc=SA)
            k.op("scalar", lambda e: e.activation(sp1[:], sp1[:], AF.Ln,
                                                  bias=one_t[:]),
                 waits=k.at(SA), inc=SA)
            k.op("scalar", lambda e, r=r, bi=bi, bcv=bcv: e.activation(
                dtt[r][:], pb[bi][:, 0:TC], AF.Relu, bias=bcv),
                waits=k.at(SA) + sn(("dtt", r)), inc=SA)
            snaps[("bank", bi)] = k.at(SA)
            k.op("vector", lambda e, r=r: e.tensor_add(dtt[r][:], dtt[r][:],
                                                       sp1[:]),
                 waits=k.at(SA), inc=SV)
            snaps["sp1"] = k.at(SV)
            k.op("vector", lambda e, ke=ke, r=r: e.tensor_mul(
                dtu[r][:], dtt[r][:], u2[ke][:]),
                waits=k.at(SV) + sn(("dtu", r)), inc=SV)
            snaps[("dttr", r)] = k.at(SV)
            for n in range(N):
                rn = n % 2
                k.op("scalar", lambda e, ke=ke, n=n, r=r, rn=rn: e.activation(
                    dA_t[rn][:], dtt[r][:], AF.Exp,
                    scale=A_t[:, ke * N + n:ke * N + n + 1]),
                    waits=sn(("dttr", r)) + sn(("dA", rn)), inc=SA)
                k.op("vector", lambda e, n=n, r=r, rn=rn: e.tensor_mul(
                    v_t[rn][:], dtu[r][:], Bb[n][:]),
                    waits=k.at(SA) + sn(("v", rn)), inc=SV)
                k.op("vector", lambda e, ke=ke, n=n, rn=rn: e.tensor_tensor_scan(
                    hC[n][:], dA_t[rn][:], v_t[rn][:],
                    carry[:, ke * N + n:ke * N + n + 1], OP.mult, OP.add),
                    waits=k.at(SV, SA) + sn(("hC", n)), inc=SV)
                snaps[("dA", rn)] = k.at(SV)
                snaps[("v", rn)] = k.at(SV)
                k.op("vector", lambda e, ke=ke, n=n: e.tensor_copy(
                    carry[:, ke * N + n:ke * N + n + 1], hC[n][:, TC - 1:TC]),
                    waits=k.at(SV), inc=SV)
                k.op("vector", lambda e, n=n: e.tensor_mul(hC[n][:], hC[n][:],
                                                           Cb[n][:]),
                    waits=k.at(SV), inc=SV)
            snaps[("dtt", r)] = k.at(SV)
            snaps[("dtu", r)] = k.at(SV)
            stride = 1
            while stride < N:
                for n in range(0, N, 2 * stride):
                    k.op("vector", lambda e, n=n, s=stride: e.tensor_add(
                        hC[n][:], hC[n][:], hC[n + s][:]), waits=k.at(SV),
                        inc=SV)
                stride *= 2
            for n in range(N):
                snaps[("hC", n)] = k.at(SV)
            k.op("vector", lambda e, ke=ke: e.scalar_tensor_tensor(
                yacc[:], u2[ke][:], Dp_t[:, ke:ke + 1], hC[0][:],
                OP.mult, OP.add), waits=k.at(SV, SA) + sn("yacc"), inc=SV)
            k.op("vector", lambda e, ke=ke, r=r: e.tensor_mul(
                y2t[ke][:], yacc[:], szt[r][:, 0:TC]),
                 waits=k.at(SV, "bszst") + sn(("y2t", ke)), inc=SV)
            snaps["yacc"] = k.at(SV)
            snaps[("szr", r)] = k.at(SV)
        k.op("vector", lambda e: e.tensor_copy(flv[:],
                                               y2t[NE - 1][:, TC - 1:TC]),
             waits=None, inc=SV)
        snaps[("ustg", 0)] = k.at(SV)
        for n in range(2 * N):
            snaps[("BC", n)] = k.at(SV)
        for ke in range(NE):
            snaps[("u2", ke)] = k.at(SV, SPE)
        snaps["dtlo"] = k.at(SPE)
        for jd in range(ND):
            bi = 2 + jd % 4
            bank = pb[bi]
            wob = wout_w[jd % 2]
            dma("sync", wob[:], wout_bf[:, jd, :],
                waits=sn(("wout", jd % 2)) + k.at("bwc"), sem=f"bwout{jd % 2}")
            gw = k.at(f"bwout{jd % 2}")
            for ke in range(NE):
                k.op("tensor", lambda e, ke=ke, bank=bank, wob=wob: e.matmul(
                    bank[:, 0:TC], wob[:, ke * 128:(ke + 1) * 128], y2t[ke][:],
                    start=(ke == 0), stop=(ke == NE - 1)),
                    waits=(gw + k.at(SV) + sn(("bank", bi)) if ke == 0 else []),
                    inc=SPE)
            snaps[("wout", jd % 2)] = k.at(SPE)
            k.op("scalar", lambda e, jd=jd, bank=bank: e.copy(
                pev[:, (jd % 2) * TC:(jd % 2) * TC + TC], bank[:, 0:TC]),
                waits=k.at(SPE) + sn(("pev", jd % 2)), inc=SA)
            snaps[("bank", bi)] = k.at(SA)
            for it in range(NT):
                bkt = 6 if it % 2 == 0 else 0
                k.op("tensor", lambda e, jd=jd, it=it, bkt=bkt: e.transpose(
                    pb[bkt][:, 0:128],
                    pev[:, (jd % 2) * TC + it * 128:(jd % 2) * TC + it * 128 + 128],
                    eye_f[:]),
                    waits=k.at(SA) + sn(("bank", bkt)), inc=SPE)
                k.op("vector", lambda e, jd=jd, it=it, bkt=bkt:
                     e.scalar_tensor_tensor(
                    xt[it][:, jd * 128:(jd + 1) * 128], pb[bkt][:, 0:128],
                    rs_b[:], xt[it][:, jd * 128:(jd + 1) * 128],
                    OP.mult, OP.add), waits=k.at(SPE), inc=SV)
                snaps[("bank", bkt)] = k.at(SV)
            snaps[("pev", jd % 2)] = k.at(SPE)
        for ke in range(NE):
            snaps[("y2t", ke)] = k.at(SPE)
        for it in range(NT):
            k.op("vector", lambda e, it=it: e.tensor_copy(
                flv[:], xt[it][:, D - 1:D]), waits=None, inc=SV)
            dma("sync", x1v[c, it], xt[it][:], waits=k.at(SV), sem=f"bxt{it}")
            layernorm_t(xt[it][:], xn0[it][:])
            k.op("scalar", lambda e, it=it: e.activation(sq[:, 0:D], xn0[it][:],
                                                         AF.Abs),
                 waits=k.at(SV) + sn("sq"), inc=SA)
            k.op("vector", lambda e: e.tensor_reduce(st1[:], sq[:, 0:D], AXX,
                                                     OP.max),
                 waits=k.at(SA), inc=SV)
            snaps["sq"] = k.at(SV)
            k.op("vector", lambda e: e.tensor_max(mx_x[:], mx_x[:], st1[:]),
                 waits=k.at(SV), inc=SV)
            k.op("vector", lambda e, it=it: e.tensor_copy(
                flv[:], xn0[it][:, D - 1:D]), waits=None, inc=SV)
            dma("sync", xn1v[c, it], xn0[it][:], waits=k.at(SV),
                sem=f"bxn{it}")
            snaps[("xt", it)] = k.at(f"bxt{it}")
            snaps[("xn0", it)] = k.at(f"bxn{it}")

    cross_part_max(mx_x[:], 1)
    dma("sync", cc_in[0].ap()[0:1, 0:2], s11[0:1, 1:3], waits=k.at(SV), sem="bcc")
    k.op("gpsimd", lambda e: e.collective_compute(
        "AllReduce", OP.max, replica_groups=[list(range(8))],
        ins=[cc_in[0].ap().opt()], outs=[cc_out[0].ap().opt()]),
        waits=k.at("bcc"), inc="cc0")
    dma("sync", s11[0:1, 2:3], cc_out[0].ap()[0:1, 0:1], waits=k.at("cc0"),
        sem="bcc")
    k.op("vector", lambda e: e.tensor_scalar(
        s11[0:1, 2:3], s11[0:1, 2:3], 1.0 / QA, 1e-12, OP.mult, OP.max),
        waits=k.at("bcc"), inc=SV)
    k.op("vector", lambda e: e.reciprocal(s11[0:1, 3:4], s11[0:1, 2:3]),
         waits=k.at(SV), inc=SV)
    bcast128(s11[0:1, 3:4], scl["rx"][:])

    k.op("sync", lambda e: e.nop(), waits=k.at("bfb2"))
    for c in range(NCH):
        for it in range(NT):
            dma("sync", xt[it][:], xn1v[c, it], waits=sn(("xt", it)),
                sem=f"bxt{it}")
        ldx1 = k.at("bxt0", "bxt1", "bxt2", "bxt3",
                    "bxn0", "bxn1", "bxn2", "bxn3")
        snaps["xiT_ready"] = None
        for it in range(NT):
            k.op("vector", lambda e, it=it: e.tensor_scalar(
                fi1[:, 0:D], xt[it][:], scl["rx"][:], None, OP.mult),
                waits=ldx1 + k.at(SA) + sn("fi1"), inc=SV)
            snaps[("xt", it)] = k.at(SV)
            k.op("vector", lambda e: e.tensor_copy(fb1[:, 0:D], fi1[:, 0:D]),
                 waits=k.at(SV) + sn("fb1"), inc=None)
            k.op("vector", lambda e: e.tensor_copy(st3[:], fb1[:, D - 1:D]),
                 waits=None, inc=SV)
            snaps["fi1"] = k.at(SV)
            for jd in range(ND + 1):
                if jd < ND:
                    k.op("tensor", lambda e, jd=jd: e.transpose(
                        pt[:, (jd % 2) * 128:(jd % 2) * 128 + 128],
                        fb1[:, jd * 128:(jd + 1) * 128], eye_b[:]),
                        waits=k.at(SV) + sn(("bank", "pt", jd % 2)), inc=SPE)
                if jd > 0:
                    pj = jd - 1
                    k.op("scalar", lambda e, pj=pj, it=it: e.copy(
                        xnT[pj][:, it * 128:(it + 1) * 128],
                        pt[:, (pj % 2) * 128:(pj % 2) * 128 + 128]),
                        waits=k.at(SPE) + (sn(("xnT", pj)) if it == 0 else []),
                        inc=SA)
                    snaps[("bank", "pt", pj % 2)] = k.at(SA)
            snaps["fb1"] = k.at(SPE)
            k.op("scalar", lambda e, it=it: e.copy(
                fla[:], xnT[ND - 1][:, it * 128 + 127:it * 128 + 128]),
                waits=None, inc=SA)
        snaps["xiT_ready"] = k.at(SA)
        for im, (wsp, osp, mxt) in enumerate(
                [(wu_sp, up_sp, mxa), (wg_sp, gate_sp, mxb)]):
            for ii in range(NI):
                bi = 2 + ii % 4
                bank = pb[bi]
                wb = wub[ii % 2]
                dma("sync", wb[:], wsp[ii],
                    waits=sn(("wub", ii % 2)), sem=f"bwub{ii % 2}")
                for kd in range(ND):
                    k.op("tensor", lambda e, wb=wb, kd=kd, bank=bank: e.matmul(
                        bank[:, 0:TC], wb[:, kd * 128:(kd + 1) * 128],
                        xnT[kd][:, 0:TC],
                        start=(kd == 0), stop=(kd == ND - 1)),
                        waits=(k.at(f"bwub{ii % 2}") + sn("xiT_ready")
                               + sn(("bank", bi)) if kd == 0 else []),
                        inc=SPE)
                snaps[("wub", ii % 2)] = k.at(SPE)
                k.op("scalar", lambda e, bank=bank: e.copy(fb2[:, 0:TC],
                                                           bank[:, 0:TC]),
                     waits=k.at(SPE) + sn("fb2"), inc=SA)
                snaps[("bank", bi)] = k.at(SA)
                k.op("scalar", lambda e: e.activation(
                    sp1[:, 0:TC], fb2[:, 0:TC], AF.Abs),
                    waits=k.at(SV), inc=SA)
                dma("gpsimd", osp[ii, :, c * TC:(c + 1) * TC], fb2[:, 0:TC],
                    waits=k.at(SA), sem="bfb2")
                snaps["fb2"] = k.at("bfb2", SA)
                k.op("vector", lambda e: e.tensor_reduce(
                    st1[:], sp1[:, 0:TC], AXX, OP.max),
                    waits=k.at(SA), inc=SV)
                k.op("vector", lambda e, mxt=mxt: e.tensor_max(mxt[:], mxt[:],
                                                               st1[:]),
                     waits=k.at(SV), inc=SV)
        for jd in range(ND):
            snaps[("xnT", jd)] = k.at(SPE)

    cross_part_max(mxa[:], 4)
    cross_part_max(mxb[:], 5)
    dma("sync", cc_in[1].ap()[0:1, 0:2], s11[0:1, 4:6], waits=k.at(SV), sem="bcc")
    k.op("gpsimd", lambda e: e.collective_compute(
        "AllReduce", OP.max, replica_groups=[list(range(8))],
        ins=[cc_in[1].ap().opt()], outs=[cc_out[1].ap().opt()]),
        waits=k.at("bcc"), inc="cc1")
    dma("sync", s11[0:1, 4:6], cc_out[1].ap()[0:1, 0:2], waits=k.at("cc1"),
        sem="bcc")
    for src, dst in ((4, 6), (5, 7)):
        k.op("vector", lambda e, src=src, dst=dst: e.tensor_mul(
            s11[0:1, dst:dst + 1], s11[0:1, src:src + 1], s11[0:1, 2:3]),
            waits=k.at("bcc", SV), inc=SV)
        k.op("vector", lambda e, dst=dst: e.tensor_scalar(
            s11[0:1, dst:dst + 1], s11[0:1, dst:dst + 1], 1.0 / QA, 1e-12,
            OP.mult, OP.max), waits=k.at(SV), inc=SV)
    k.op("vector", lambda e: e.reciprocal(s11[0:1, 8:9], s11[0:1, 6:7]),
         waits=k.at(SV), inc=SV)
    k.op("vector", lambda e: e.tensor_mul(s11[0:1, 8:9], s11[0:1, 8:9],
                                          s11[0:1, 2:3]), waits=k.at(SV), inc=SV)
    k.op("vector", lambda e: e.reciprocal(s11[0:1, 9:10], s11[0:1, 7:8]),
         waits=k.at(SV), inc=SV)
    k.op("vector", lambda e: e.tensor_mul(s11[0:1, 9:10], s11[0:1, 9:10],
                                          s11[0:1, 2:3]), waits=k.at(SV), inc=SV)
    bcast128(s11[0:1, 8:9], scl["c2"][:])
    bcast128(s11[0:1, 9:10], scl["c1"][:])
    bcast128(s11[0:1, 7:8], scl["sg"][:])

    k.op("vector", lambda e: e.memset(mxa[:], 0.0), waits=k.at(SV), inc=SV)
    k.op("sync", lambda e: e.nop(), waits=k.at("bfb2"))
    otiles = hC + y2t
    for c in range(NCH):
        for ii in range(NI):
            dma("sync", otiles[ii][:], gate_sp[ii, :, c * TC:(c + 1) * TC],
                waits=sn(("ot", ii)) + k.at("bsp"), sem="bot")
        ldg = k.at("bot")
        for ii in range(NI):
            ot = otiles[ii]
            k.op("vector", lambda e, ot=ot: e.tensor_scalar(
                fi1[:, 0:TC], ot[:], scl["c1"][:], None, OP.mult),
                waits=ldg + k.at(SA) + sn("fi1"), inc=SV)
            k.op("scalar", lambda e: e.activation(sp1[:, 0:TC], fi1[:, 0:TC],
                                                  AF.Sigmoid, scale=scl["sg"][:]),
                 waits=k.at(SV) + sn("ff1"), inc=SA)
            k.op("vector", lambda e: e.tensor_scalar(
                fi1[:, TC:2 * TC], sp1[:, 0:TC], 65536.0, 65535.0,
                OP.mult, OP.min), waits=k.at(SA), inc=SV)
            snaps["ff1"] = k.at(SV)
            k.op("vector", lambda e, ot=ot: e.tensor_tensor(
                ot[:], fi1[:, 0:TC], fi1[:, TC:2 * TC], OP.mult),
                waits=k.at(SV), inc=SV)
            snaps["fi1"] = k.at(SV)
            k.op("vector", lambda e, ot=ot: e.tensor_copy(
                flv[:], ot[:, TC - 1:TC]), waits=None, inc=SV)
            dma("gpsimd", g2p_sp[ii, :, c * TC:(c + 1) * TC], ot[:],
                waits=k.at(SV), sem="bsp")
            k.op("scalar", lambda e, ot=ot: e.activation(sp1[:, 0:TC],
                                                  ot[:], AF.Abs),
                 waits=k.at(SV, SA), inc=SA)
            snaps[("ot", ii)] = k.at("bot", "bsp", SA)
            k.op("vector", lambda e: e.tensor_reduce(st1[:], sp1[:, 0:TC],
                                                     AXX, OP.max),
                 waits=k.at(SA), inc=SV)
            k.op("vector", lambda e: e.tensor_max(mxa[:], mxa[:], st1[:]),
                 waits=k.at(SV), inc=SV)

    cross_part_max(mxa[:], 10)
    dma("sync", cc_in[2].ap()[0:1, 0:2], s11[0:1, 10:12], waits=k.at(SV), sem="bcc")
    k.op("gpsimd", lambda e: e.collective_compute(
        "AllReduce", OP.max, replica_groups=[list(range(8))],
        ins=[cc_in[2].ap().opt()], outs=[cc_out[2].ap().opt()]),
        waits=k.at("bcc"), inc="cc2")
    dma("sync", s11[0:1, 10:11], cc_out[2].ap()[0:1, 0:1], waits=k.at("cc2"),
        sem="bcc")
    k.op("vector", lambda e: e.tensor_scalar(
        s11[0:1, 11:12], s11[0:1, 7:8], 1.0 / 65536.0, None, OP.mult),
        waits=k.at("bcc", SV), inc=SV)
    k.op("vector", lambda e: e.tensor_mul(s11[0:1, 12:13], s11[0:1, 11:12],
                                          s11[0:1, 10:11]),
         waits=k.at(SV), inc=SV)
    k.op("vector", lambda e: e.tensor_scalar(
        s11[0:1, 12:13], s11[0:1, 12:13], 1.0 / QA, 1e-12, OP.mult, OP.max),
        waits=k.at(SV), inc=SV)
    k.op("vector", lambda e: e.reciprocal(s11[0:1, 13:14], s11[0:1, 12:13]),
         waits=k.at(SV), inc=SV)
    k.op("vector", lambda e: e.tensor_mul(s11[0:1, 13:14], s11[0:1, 13:14],
                                          s11[0:1, 11:12]),
         waits=k.at(SV), inc=SV)
    bcast128(s11[0:1, 13:14], scl["c3"][:])

    k.op("vector", lambda e: e.memset(mxb[:], 0.0), waits=k.at(SV), inc=SV)
    k.op("sync", lambda e: e.nop(), waits=k.at("bfb2", "bot", "bsp"))
    for c in range(NCH):
        for half in range(2):
            i0 = half * 12
            for j in range(12):
                ii = i0 + j
                dma("sync", otiles[j][:], g2p_sp[ii, :, c * TC:(c + 1) * TC],
                    waits=sn(("ot", j)) + k.at("bsp"), sem="bot")
                dma("sync", otiles[12 + j][:],
                    up_sp[ii, :, c * TC:(c + 1) * TC],
                    waits=sn(("ot", 12 + j)), sem="bot")
            ldg = k.at("bot")
            for j in range(12):
                ii = i0 + j
                g2b, upb = otiles[j], otiles[12 + j]
                k.op("vector", lambda e, g2b=g2b: e.tensor_scalar(
                    fi1[:, 0:TC], g2b[:], scl["c3"][:], None, OP.mult),
                    waits=ldg + k.at(SA) + sn("fi1"), inc=SV)
                k.op("vector", lambda e, upb=upb: e.tensor_scalar(
                    fi1[:, TC:2 * TC], upb[:], scl["c2"][:], None,
                    OP.mult), waits=k.at(SV) + sn("fi1b"), inc=SV)
                k.op("vector", lambda e, g2b=g2b: e.tensor_tensor(
                    g2b[:], fi1[:, 0:TC], fi1[:, TC:2 * TC], OP.mult),
                    waits=k.at(SV), inc=SV)
                snaps["fi1"] = k.at(SV)
                snaps["fi1b"] = k.at(SV)
                snaps[("ot", 12 + j)] = k.at(SV)
                k.op("vector", lambda e, g2b=g2b: e.tensor_copy(
                    flv[:], g2b[:, TC - 1:TC]), waits=None, inc=SV)
                dma("gpsimd", o_sp[ii, :, c * TC:(c + 1) * TC], g2b[:],
                    waits=k.at(SV), sem="bsp")
                k.op("scalar", lambda e, g2b=g2b: e.activation(
                    sp1[:, 0:TC], g2b[:], AF.Abs),
                    waits=k.at(SV, SA), inc=SA)
                snaps[("ot", j)] = k.at("bot", "bsp", SA)
                k.op("vector", lambda e: e.tensor_reduce(st1[:], sp1[:, 0:TC],
                                                         AXX, OP.max),
                     waits=k.at(SA), inc=SV)
                k.op("vector", lambda e: e.tensor_max(mxb[:], mxb[:], st1[:]),
                     waits=k.at(SV), inc=SV)

    cross_part_max(mxb[:], 14)
    dma("sync", cc_in[3].ap()[0:1, 0:2], s11[0:1, 14:16], waits=k.at(SV), sem="bcc")
    k.op("gpsimd", lambda e: e.collective_compute(
        "AllReduce", OP.max, replica_groups=[list(range(8))],
        ins=[cc_in[3].ap().opt()], outs=[cc_out[3].ap().opt()]),
        waits=k.at("bcc"), inc="cc3")
    dma("sync", s11[0:1, 14:15], cc_out[3].ap()[0:1, 0:1], waits=k.at("cc3"),
        sem="bcc")
    k.op("vector", lambda e: e.tensor_mul(s11[0:1, 15:16], s11[0:1, 12:13],
                                          s11[0:1, 6:7]),
         waits=k.at("bcc", SV), inc=SV)
    k.op("vector", lambda e: e.tensor_mul(s11[0:1, 16:17], s11[0:1, 15:16],
                                          s11[0:1, 14:15]),
         waits=k.at(SV), inc=SV)
    k.op("vector", lambda e: e.tensor_scalar(
        s11[0:1, 16:17], s11[0:1, 16:17], 1.0 / QA, 1e-12, OP.mult, OP.max),
        waits=k.at(SV), inc=SV)
    k.op("vector", lambda e: e.reciprocal(s11[0:1, 17:18], s11[0:1, 16:17]),
         waits=k.at(SV), inc=SV)
    k.op("vector", lambda e: e.tensor_mul(s11[0:1, 17:18], s11[0:1, 17:18],
                                          s11[0:1, 15:16]),
         waits=k.at(SV), inc=SV)
    bcast128(s11[0:1, 17:18], scl["c4"][:])
    bcast128(s11[0:1, 16:17], scl["so"][:])

    k.op("sync", lambda e: e.nop(), waits=k.at("bfb2", "bot", "bsp"))
    for c in range(NCH):
        for it in range(NT):
            dma("sync", xt[it][:], x1v[c, it], waits=sn(("xt", it)),
                sem=f"bxt{it}")
        for ii in range(NI):
            dma("sync", otiles[ii][:], o_sp[ii, :, c * TC:(c + 1) * TC],
                waits=sn(("ot", ii)) + k.at("bsp"), sem="bot")
        ldot = k.at("bot", SA)
        for ii in range(NI):
            ot = otiles[ii]
            k.op("vector", lambda e, ot=ot: e.tensor_scalar(
                fi1[:, 0:TC], ot[:], scl["c4"][:], None, OP.mult),
                waits=ldot + sn("fi1"), inc=SV)
            k.op("vector", lambda e, ot=ot: e.tensor_copy(ot[:], fi1[:, 0:TC]),
                 waits=k.at(SV), inc=SV)
            snaps["fi1"] = k.at(SV)
        for jd in range(ND):
            bi = 2 + jd % 4
            bank = pb[bi]
            wb = wdb[0]
            dma("sync", wb, wd_sp[jd],
                waits=sn(("wdb", 0)) + sn(("wxz", 0)), sem="bwdb")
            for ki in range(NI):
                k.op("tensor", lambda e, wb=wb, ki=ki, bank=bank: e.matmul(
                    bank[:, 0:TC], wb[:, ki * 128:(ki + 1) * 128],
                    otiles[ki][:],
                    start=(ki == 0), stop=(ki == NI - 1)),
                    waits=(k.at("bwdb", SV) + sn(("bank", bi)) if ki == 0
                           else []), inc=SPE)
            snaps[("wdb", 0)] = k.at(SPE)
            k.op("scalar", lambda e, jd=jd, bank=bank: e.copy(
                pev[:, (jd % 2) * TC:(jd % 2) * TC + TC], bank[:, 0:TC]),
                waits=k.at(SPE) + sn(("pev", jd % 2)), inc=SA)
            snaps[("bank", bi)] = k.at(SA)
            for it in range(NT):
                bkt = 6 if it % 2 == 0 else 0
                k.op("tensor", lambda e, jd=jd, it=it, bkt=bkt: e.transpose(
                    pb[bkt][:, 0:128],
                    pev[:, (jd % 2) * TC + it * 128:(jd % 2) * TC + it * 128 + 128],
                    eye_f[:]),
                    waits=k.at(SA) + sn(("bank", bkt)), inc=SPE)
                k.op("vector", lambda e, jd=jd, it=it, bkt=bkt:
                     e.scalar_tensor_tensor(
                    xt[it][:, jd * 128:(jd + 1) * 128], pb[bkt][:, 0:128],
                    scl["so"][:], xt[it][:, jd * 128:(jd + 1) * 128],
                    OP.mult, OP.add),
                    waits=k.at(SPE, SA, "bxt0", "bxt1", "bxt2", "bxt3"),
                    inc=SV)
                snaps[("bank", bkt)] = k.at(SV)
            snaps[("pev", jd % 2)] = k.at(SPE)
        for ii in range(NI):
            snaps[("ot", ii)] = k.at(SPE)
        for it in range(NT):
            dma("sync", outv[c, it], xt[it][:], waits=k.at(SV), sem=SO)
            snaps[("xt", it)] = k.at(SO)

    k.op("sync", lambda e: e.wait_ge(k.sems[SO], k.cnt[SO]))

    with nc.Block() as blk:
        k.emit(blk)
    return nc


_NC_CACHE = {}


def kernel(L0_attn_x, L0_attn_ln_g, L0_attn_ln_b, L0_attn_W_in, L0_attn_conv_w,
           L0_attn_conv_b, L0_attn_W_xp, L0_attn_W_dt, L0_attn_b_dt,
           L0_attn_A_log, L0_attn_D_param, L0_attn_W_out, L0_attn_res_scale,
           L0_ffn_w_gate, L0_ffn_w_up, L0_ffn_w_down):
    B = L0_attn_x.shape[0]
    f32 = np.float32
    asc = np.ascontiguousarray

    gcol = asc(L0_attn_ln_g.reshape(ND, 128).T.astype(f32))
    bcol = asc(L0_attn_ln_b.reshape(ND, 128).T.astype(f32))
    win = asc(L0_attn_W_in.reshape(ND, 128, 8, 4, 128).transpose(2, 1, 0, 3, 4)
              .reshape(8, 128, ND * 4 * 128).astype(f32))
    cw = asc(L0_attn_conv_w.reshape(NE, 128, 4).transpose(1, 0, 2)
             .reshape(128, NE * 4).astype(f32))
    cb = asc(L0_attn_conv_b.reshape(NE, 128).T.astype(f32))
    wxp = asc(L0_attn_W_xp.reshape(NE, 128, RK + 2 * N).transpose(1, 0, 2)
              .reshape(128, NE * (RK + 2 * N)).astype(f32))
    bdt = asc(L0_attn_b_dt.reshape(NE, 128).T.astype(f32))
    alog = asc(L0_attn_A_log.reshape(NE, 128, N).transpose(1, 0, 2)
               .reshape(128, NE * N).astype(f32))
    dpar = asc(L0_attn_D_param.reshape(NE, 128).T.astype(f32))
    wout = asc(L0_attn_W_out.reshape(NE, 128, ND, 128).transpose(1, 2, 0, 3)
               .reshape(128, ND, NE * 128).astype(f32))
    eye = np.eye(128, dtype=f32)
    sel32 = np.zeros((32, 32 * 128), dtype=f32)
    for n in range(32):
        sel32[n, n * 128:(n + 1) * 128] = 1.0

    if "nc" not in _NC_CACHE:
        _NC_CACHE["nc"] = build_program()
    nc = _NC_CACHE["nc"]

    common = dict(gcol=gcol, bcol=bcol, win=win, cw=cw, cb=cb, wxp=wxp,
                  wdt=asc(L0_attn_W_dt.astype(f32)), bdt=bdt, alog=alog,
                  dpar=dpar, wout=wout,
                  res_scale=asc(L0_attn_res_scale.reshape(1, 1).astype(f32)),
                  w_gate=asc(L0_ffn_w_gate.astype(f32)),
                  w_up=asc(L0_ffn_w_up.astype(f32)),
                  w_down=asc(L0_ffn_w_down.astype(f32)),
                  eye=eye, sel32=sel32)
    in_maps = []
    for i in range(B):
        m = dict(common)
        m["x"] = asc(L0_attn_x[i].astype(f32))
        in_maps.append(m)
    res = run_bass_kernel_spmd(nc, in_maps, core_ids=list(range(B)))
    out = np.stack([np.asarray(res.results[i]["out"], dtype=np.float32)
                    for i in range(B)], axis=0)
    return out
